# version 02
# baseline (speedup 1.0000x reference)
import numpy as np
import ml_dtypes
from contextlib import ExitStack
import concourse.bass as bass
import concourse.mybir as mybir
from concourse.bass_utils import run_bass_kernel_spmd

F32 = mybir.dt.float32
BF16 = mybir.dt.bfloat16
AX = mybir.AxisListType.X
AF = mybir.ActivationFunctionType
ALU = mybir.AluOpType

T = 5632
NTT = T // 128
NCH = T // 512
D = 1024
EPS = 1e-5
NEG = -1.0e30


class Buf:
    __slots__ = ("w", "r")

    def __init__(self):
        self.w = None
        self.r = []


class Sched:
    ENGS = ("sync", "gpsimd", "tensor", "vector", "scalar")

    def __init__(self):
        self.ops = {e: [] for e in self.ENGS}
        self.cnt = {e: 0 for e in self.ENGS}

    def op(self, eng, m, *args, reads=(), writes=(), deps=(), **kw):
        dd = []
        for b in reads:
            if b.w is not None:
                dd.append(b.w)
        for b in writes:
            dd += b.r
            if b.w is not None:
                dd.append(b.w)
        dd += list(deps)
        inc = 16 if m == "dma_start" else 1
        if m == "dma_start" and self.cnt[eng] > 0:
            dd.append((eng, self.cnt[eng]))
        self.cnt[eng] += inc
        tok = (eng, self.cnt[eng])
        waits = []
        for (pe, pv) in dd:
            if pe == eng and eng == "tensor":
                continue
            waits.append((pe, pv))
        self.ops[eng].append((waits, m, args, kw, inc))
        for b in reads:
            b.r.append(tok)
        for b in writes:
            b.w = tok
            b.r = []
        return tok

    def barrier(self):
        for e in self.ENGS:
            waits = [(p, self.cnt[p]) for p in self.ENGS if p != e]
            self.ops[e].append((waits, None, (), {}, 0))

    def replay(self, block, sems):
        for eng in self.ENGS:
            ops = self.ops[eng]
            if not ops:
                continue

            def mk(eng, ops):
                def f(proxy):
                    seen = {}
                    for (waits, m, args, kw, inc) in ops:
                        for (pe, pv) in waits:
                            if seen.get(pe, 0) >= pv:
                                continue
                            proxy.wait_ge(sems[pe], pv)
                            seen[pe] = pv
                        if m is None:
                            continue
                        r = getattr(proxy, m)(*args, **kw)
                        r.then_inc(sems[eng], inc)
                return f
            getattr(block, eng)(mk(eng, ops))
        self.ops = {e: [] for e in self.ENGS}


def build_program(debug=False):
    nc = bass.Bass("TRN2", target_bir_lowering=False)
    g = lambda n, shp, dt=F32: nc.dram_tensor(n, shp, dt, kind="ExternalInput").ap()

    x_in = g("x", [T, D])
    y_out = nc.dram_tensor("y", [T, D], F32, kind="ExternalOutput").ap()

    wqkvT = g("wqkvT", [D, 3072])
    woT = g("woT", [D, D])
    g1b = g("g1b", [128, D]); b1b = g("b1b", [128, D])
    g2b = g("g2b", [128, D]); b2b = g("b2b", [128, D])
    w1T = g("w1T", [D, 4096])
    b1A = g("b1A", [128, 32])
    w2T = g("w2T", [4096, D])
    b2bc = g("b2bc", [128, D])
    maskA = g("maskA", [8, 128, 1024], BF16)
    maskF = g("maskF", [8, 128, 1024], BF16)
    identH = g("identH", [128, 128], BF16)
    rqc = g("rqc", [T, 512]); rqs = g("rqs", [T, 512])
    rkc = g("rkc", [T, 512]); rks = g("rks", [T, 512])
    fl = []
    for l in range(2):
        fl.append(dict(
            qkv=g(f"fqkv{l}", [D, 1536]),
            qs=g(f"fqs{l}", [128, 512]), ks=g(f"fks{l}", [128, 512]),
            wo=g(f"fwo{l}", [512, D]),
            lg=g(f"flg{l}", [128, D]), lb=g(f"flb{l}", [128, D]),
            w1a=g(f"fw1a{l}", [D, 2816]), w1g=g(f"fw1g{l}", [D, 2816]),
            b1a=g(f"fb1a{l}", [128, 22]), b1g=g(f"fb1g{l}", [128, 22]),
            w2=g(f"fw2{l}", [2816, D]), b2=g(f"fb2bc{l}", [128, D]),
        ))

    ikind = "ExternalOutput" if debug else "Internal"
    dscr = lambda n, shp, dt=F32: nc.dram_tensor(n, shp, dt, kind="Internal").ap()
    dscr2 = lambda n, shp, dt=F32: nc.dram_tensor(n, shp, dt, kind=ikind).ap()
    x1_d = dscr2("x1_d", [T, D])
    xA_d = dscr2("xA_d", [T, D])
    xi_d = dscr2("xi_d", [T, D])
    ya_d = dscr("ya_d", [T, D])
    z_d = dscr2("z_d", [T, D])
    m_d = dscr("m_d", [NCH, 128, 22 * 512], BF16)

    S = Sched()
    es = ExitStack()
    sems = {e: es.enter_context(nc.semaphore(f"sem_{e}")) for e in Sched.ENGS}

    def ln(t_in, t_in_buf, out_ap, out_buf, xc_ap, xc_buf, sq_ap, sq_buf, st_ap, st_buf,
           gain=None, gain_buf=None, bias=None, bias_buf=None, width=D):
        S.op("vector", "reduce_sum", st_ap[:, 0:1], t_in, AX, reads=[t_in_buf], writes=[st_buf])
        S.op("scalar", "mul", st_ap[:, 1:2], st_ap[:, 0:1], 1.0 / width, reads=[st_buf], writes=[st_buf])
        S.op("vector", "tensor_scalar_sub", xc_ap, t_in, st_ap[:, 1:2], reads=[t_in_buf, st_buf], writes=[xc_buf])
        S.op("scalar", "square", sq_ap, xc_ap, reads=[xc_buf], writes=[sq_buf])
        S.op("vector", "reduce_sum", st_ap[:, 0:1], sq_ap, AX, reads=[sq_buf], writes=[st_buf])
        S.op("scalar", "activation", st_ap[:, 1:2], st_ap[:, 0:1], AF.Sqrt, bias=CONST[:, 0:1], scale=1.0 / width,
             reads=[st_buf, CONST_b], writes=[st_buf])
        S.op("vector", "reciprocal", st_ap[:, 2:3], st_ap[:, 1:2], reads=[st_buf], writes=[st_buf])
        if gain is None:
            S.op("vector", "tensor_scalar_mul", out_ap, xc_ap, st_ap[:, 2:3], reads=[xc_buf, st_buf], writes=[out_buf])
        else:
            S.op("vector", "tensor_scalar_mul", xc_ap, xc_ap, st_ap[:, 2:3], reads=[xc_buf, st_buf], writes=[xc_buf])
            S.op("vector", "tensor_mul", xc_ap, xc_ap, gain, reads=[xc_buf, gain_buf], writes=[xc_buf])
            S.op("vector", "tensor_add", out_ap, xc_ap, bias, reads=[xc_buf, bias_buf], writes=[out_buf])

    def load_w_bf16(src_ap, n_part, kcols, dst_ap_fn, stag, stag_buf, dst_buf, cap=4096):
        for pc in range(n_part):
            dst = dst_ap_fn(pc)
            for c0 in range(0, kcols, cap):
                c1 = min(kcols, c0 + cap)
                S.op("sync", "dma_start", out=stag[:, 0:c1 - c0], in_=src_ap[pc * 128:(pc + 1) * 128, c0:c1],
                     reads=[], writes=[stag_buf])
                S.op("scalar", "copy", dst[:, c0:c1], stag[:, 0:c1 - c0], reads=[stag_buf], writes=[dst_buf])

    def tps(in_ap, in_buf, out_ap, out_buf, tp, tp_bufs, idx, ident_ap, ident_buf):
        k = idx % 2
        fc = in_ap.shape[1]
        pc = in_ap.shape[0]
        S.op("tensor", "transpose", tp[k][0:fc, 0:pc], in_ap, ident_ap,
             reads=[in_buf, ident_buf], writes=[tp_bufs[k]])
        S.op("scalar", "copy", out_ap, tp[k][0:fc, 0:pc], reads=[tp_bufs[k]], writes=[out_buf])

    ident_t = es.enter_context(nc.sbuf_tensor("ident", [128, 128], BF16))
    ident_b = Buf()
    S.op("sync", "dma_start", out=ident_t[:], in_=identH[:, :], reads=[], writes=[ident_b])
    CONST = es.enter_context(nc.sbuf_tensor("CONST", [128, 2], F32))
    CONST_b = Buf()
    S.op("gpsimd", "memset", CONST[:, 0:1], EPS, reads=[], writes=[CONST_b])
    S.op("gpsimd", "memset", CONST[:, 1:2], 1e-24, reads=[], writes=[CONST_b])

    with nc.Block() as block:

        with ExitStack() as ph:
            sb = lambda n, shp, dt=F32: ph.enter_context(nc.sbuf_tensor(n, shp, dt))
            ps = lambda n, shp, dt=F32: ph.enter_context(nc.psum_tensor(n, shp, dt))
            WQ = sb("WQ", [128, 8 * 3072], BF16); WQ_b = Buf()
            WO = sb("WOp", [128, 8 * 1024], BF16); WO_b = Buf()
            MSK = sb("MSK", [128, 8 * 1024], BF16); MSK_b = Buf()
            G1 = sb("G1", [128, D]); G1_b = Buf()
            B1 = sb("B1", [128, D]); B1_b = Buf()
            with ExitStack() as wl:
                stag = wl.enter_context(nc.sbuf_tensor("stg1", [128, 3072], F32)); stag_b = Buf()
                load_w_bf16(wqkvT, 8, 3072, lambda pc: WQ[:, pc * 3072:(pc + 1) * 3072], stag, stag_b, WQ_b)
                load_w_bf16(woT, 8, 1024, lambda pc: WO[:, pc * 1024:(pc + 1) * 1024], stag, stag_b, WO_b)
                for i in range(8):
                    S.op("sync", "dma_start", out=MSK[:, i * 1024:(i + 1) * 1024], in_=maskA[i, :, :],
                         reads=[], writes=[MSK_b])
                S.op("sync", "dma_start", out=G1[:], in_=g1b[:, :], reads=[], writes=[G1_b])
                S.op("sync", "dma_start", out=B1[:], in_=b1b[:, :], reads=[], writes=[B1_b])
                S.barrier()

            xinner = sb("xinner", [128, 4 * D]); xinner_b = Buf()
            xc = sb("xc", [128, D]); xc_b = Buf()
            stt = sb("stt", [128, 4]); stt_b = Buf()
            h1 = sb("h1", [128, D], BF16); h1_b = Buf()
            h1FM = sb("h1FM", [128, 8 * 512], BF16); h1FM_b = Buf()
            qFM = sb("qFM", [128, 8 * 512], BF16); qFM_b = Buf()
            kFM = sb("kFM", [128, 2 * 8 * 512], BF16); kFM_b = Buf()
            vTM = sb("vTM", [128, 2 * 4 * 1024], BF16); vTM_b = Buf()
            sm = sb("sm", [128, 1024]); sm_b = Buf()
            ebf = sb("ebf", [128, 1024], BF16); ebf_b = Buf()
            aT = sb("aT", [128, 1024], BF16); aT_b = Buf()
            ao = sb("ao", [128, 4 * 1024], BF16); ao_b = Buf()
            aoFM = sb("aoFM", [128, 8 * 512], BF16); aoFM_b = Buf()
            x1sb = sb("x1sb", [128, D]); x1sb_b = Buf()
            MM = ps("MM", [128, 512], F32); MM_b = Buf()
            SS = ps("SS", [128, 1024], F32); SS_b = Buf()
            OO = ps("OO", [128, 512], F32); OO_b = Buf()
            TP = [ps("TP0", [128, 128], BF16), ps("TP1", [128, 128], BF16)]
            TP_b = [Buf(), Buf()]

            S.op("gpsimd", "memset", kFM[:], 0.0, reads=[], writes=[kFM_b])
            S.op("gpsimd", "memset", vTM[:], 0.0, reads=[], writes=[vTM_b])

            for c in range(NCH):
                slot = c % 2
                pslot = (c - 1) % 2
                for tt in range(4):
                    row = c * 512 + tt * 128
                    S.op("sync", "dma_start", out=xinner[:, tt * D:(tt + 1) * D], in_=x_in[row:row + 128, :],
                         reads=[], writes=[xinner_b])
                    ln(xinner[:, tt * D:(tt + 1) * D], xinner_b, xinner[:, tt * D:(tt + 1) * D], xinner_b,
                       xc[:], xc_b, sm[:], sm_b, stt[:], stt_b)
                    ln(xinner[:, tt * D:(tt + 1) * D], xinner_b, h1[:], h1_b, xc[:], xc_b, sm[:], sm_b, stt[:], stt_b,
                       gain=G1[:], gain_buf=G1_b, bias=B1[:], bias_buf=B1_b)
                    for fc in range(8):
                        tps(h1[:, fc * 128:(fc + 1) * 128], h1_b, h1FM[:, fc * 512 + tt * 128:fc * 512 + (tt + 1) * 128],
                            h1FM_b, TP, TP_b, fc, ident_t[:], ident_b)
                for oc in range(16):
                    dk = qFM if oc < 8 else kFM
                    dkb = qFM_b if oc < 8 else kFM_b
                    col = oc * 512 if oc < 8 else slot * 4096 + (oc - 8) * 512
                    for fc in range(8):
                        S.op("tensor", "matmul", MM[:], WQ[:, fc * 3072 + oc * 128:fc * 3072 + (oc + 1) * 128],
                             h1FM[:, fc * 512:(fc + 1) * 512], start=(fc == 0), stop=(fc == 7),
                             reads=[WQ_b, h1FM_b], writes=[MM_b])
                    if oc < 8:
                        S.op("scalar", "mul", dk[:, col:col + 512], MM[:], 0.125, reads=[MM_b], writes=[dkb])
                    else:
                        S.op("scalar", "copy", dk[:, col:col + 512], MM[:], reads=[MM_b], writes=[dkb])
                for tt in range(4):
                    for hf in range(2):
                        for fc in range(8):
                            S.op("tensor", "matmul", MM[:], h1FM[:, fc * 512 + tt * 128:fc * 512 + (tt + 1) * 128],
                                 WQ[:, fc * 3072 + 2048 + hf * 512:fc * 3072 + 2048 + (hf + 1) * 512],
                                 start=(fc == 0), stop=(fc == 7), reads=[h1FM_b, WQ_b], writes=[MM_b])
                        S.op("scalar", "copy", vTM[:, slot * 4096 + tt * 1024 + hf * 512: slot * 4096 + tt * 1024 + (hf + 1) * 512],
                             MM[:], reads=[MM_b], writes=[vTM_b])
                mset = 4 if c == 0 else 0
                for h in range(16):
                    hp = h // 2
                    po = (h % 2) * 64
                    for r in range(4):
                        S.op("tensor", "matmul", SS[:, 0:512], qFM[po:po + 64, hp * 512 + r * 128:hp * 512 + (r + 1) * 128],
                             kFM[po:po + 64, pslot * 4096 + hp * 512:pslot * 4096 + (hp + 1) * 512],
                             start=True, stop=True, reads=[qFM_b, kFM_b], writes=[SS_b])
                        S.op("tensor", "matmul", SS[:, 512:1024], qFM[po:po + 64, hp * 512 + r * 128:hp * 512 + (r + 1) * 128],
                             kFM[po:po + 64, slot * 4096 + hp * 512:slot * 4096 + (hp + 1) * 512],
                             start=True, stop=True, reads=[qFM_b, kFM_b], writes=[SS_b])
                        mc = (mset + r) * 1024
                        S.op("vector", "tensor_add", sm[:, 0:512], SS[:, 0:512], MSK[:, mc:mc + 512],
                             reads=[SS_b, MSK_b], writes=[sm_b])
                        S.op("vector", "tensor_add", sm[:, 512:1024], SS[:, 512:1024], MSK[:, mc + 512:mc + 1024],
                             reads=[SS_b, MSK_b], writes=[sm_b])
                        S.op("vector", "reduce_max", stt[:, 0:1], sm[:], AX, reads=[sm_b], writes=[stt_b])
                        S.op("scalar", "mul", stt[:, 1:2], stt[:, 0:1], -1.0, reads=[stt_b], writes=[stt_b])
                        S.op("scalar", "activation", ebf[:], sm[:], AF.Exp, bias=stt[:, 1:2],
                             reads=[sm_b, stt_b], writes=[ebf_b])
                        S.op("vector", "reduce_sum", stt[:, 2:3], ebf[:], AX, reads=[ebf_b], writes=[stt_b])
                        S.op("vector", "reciprocal", stt[:, 3:4], stt[:, 2:3], reads=[stt_b], writes=[stt_b])
                        for kk in range(8):
                            tps(ebf[:, kk * 128:(kk + 1) * 128], ebf_b, aT[:, kk * 128:(kk + 1) * 128], aT_b,
                                TP, TP_b, kk, ident_t[:], ident_b)
                        for kk in range(8):
                            vcol = (pslot if kk < 4 else slot) * 4096 + (kk % 4) * 1024 + h * 64
                            S.op("tensor", "matmul", OO[:, 0:64], aT[:, kk * 128:(kk + 1) * 128],
                                 vTM[:, vcol:vcol + 64], start=(kk == 0), stop=(kk == 7),
                                 reads=[aT_b, vTM_b], writes=[OO_b])
                        S.op("vector", "tensor_scalar_mul", ao[:, r * 1024 + h * 64:r * 1024 + (h + 1) * 64],
                             OO[:, 0:64], stt[:, 3:4], reads=[OO_b, stt_b], writes=[ao_b])
                for tt in range(4):
                    for fc in range(8):
                        tps(ao[:, tt * 1024 + fc * 128:tt * 1024 + (fc + 1) * 128], ao_b,
                            aoFM[:, fc * 512 + tt * 128:fc * 512 + (tt + 1) * 128], aoFM_b,
                            TP, TP_b, fc, ident_t[:], ident_b)
                for tt in range(4):
                    for hf in range(2):
                        for fc in range(8):
                            S.op("tensor", "matmul", MM[:], aoFM[:, fc * 512 + tt * 128:fc * 512 + (tt + 1) * 128],
                                 WO[:, fc * 1024 + hf * 512:fc * 1024 + (hf + 1) * 512],
                                 start=(fc == 0), stop=(fc == 7), reads=[aoFM_b, WO_b], writes=[MM_b])
                        S.op("vector", "tensor_add", x1sb[:, hf * 512:(hf + 1) * 512],
                             xinner[:, tt * 1024 + hf * 512:tt * 1024 + (hf + 1) * 512], MM[:],
                             reads=[xinner_b, MM_b], writes=[x1sb_b])
                    row = c * 512 + tt * 128
                    S.op("gpsimd", "dma_start", out=x1_d[row:row + 128, :], in_=x1sb[:], reads=[x1sb_b], writes=[])
            S.barrier()
            S.replay(block, sems)

        with ExitStack() as ph:
            sb = lambda n, shp, dt=F32: ph.enter_context(nc.sbuf_tensor(n, shp, dt))
            ps = lambda n, shp, dt=F32: ph.enter_context(nc.psum_tensor(n, shp, dt))
            W1 = sb("W1", [128, 8 * 4096], BF16); W1_b = Buf()
            W2 = sb("W2", [128, 32 * 1024], BF16); W2_b = Buf()
            G2 = sb("G2", [128, D]); G2_b = Buf()
            B2 = sb("B2", [128, D]); B2_b = Buf()
            B1a = sb("B1a", [128, 32]); B1a_b = Buf()
            B2c = sb("B2c", [128, D]); B2c_b = Buf()
            with ExitStack() as wl:
                stag = wl.enter_context(nc.sbuf_tensor("stg2", [128, 2048], F32)); stag_b = Buf()
                load_w_bf16(w1T, 8, 4096, lambda pc: W1[:, pc * 4096:(pc + 1) * 4096], stag, stag_b, W1_b, cap=2048)
                load_w_bf16(w2T, 32, 1024, lambda pc: W2[:, pc * 1024:(pc + 1) * 1024], stag, stag_b, W2_b, cap=2048)
                S.op("sync", "dma_start", out=G2[:], in_=g2b[:, :], reads=[], writes=[G2_b])
                S.op("sync", "dma_start", out=B2[:], in_=b2b[:, :], reads=[], writes=[B2_b])
                S.op("sync", "dma_start", out=B1a[:], in_=b1A[:, :], reads=[], writes=[B1a_b])
                S.op("sync", "dma_start", out=B2c[:], in_=b2bc[:, :], reads=[], writes=[B2c_b])
                S.barrier()
            x1t = sb("x1t", [128, 2 * D]); x1t_b = Buf()
            xAt = sb("xAt", [128, 2 * D]); xAt_b = Buf()
            xit = sb("xit", [128, D]); xit_b = Buf()
            xc = sb("xc2", [128, D]); xc_b = Buf()
            sq = sb("sq2", [128, D]); sq_b = Buf()
            stt = sb("stt2", [128, 4]); stt_b = Buf()
            h2 = sb("h2", [128, D], BF16); h2_b = Buf()
            h2FM = sb("h2FM", [128, 8 * 256], BF16); h2FM_b = Buf()
            h3FM = sb("h3FM", [128, 32 * 256], BF16); h3FM_b = Buf()
            M1 = ps("M1", [128, 256], F32); M1_b = Buf()
            M2 = ps("M2", [128, 512], F32); M2_b = Buf()
            TP = [ps("TP0b", [128, 128], BF16), ps("TP1b", [128, 128], BF16)]
            TP_b = [Buf(), Buf()]
            for it in range(T // 256):
                row = it * 256
                S.op("sync", "dma_start", out=x1t[:, 0:D], in_=x1_d[row:row + 128, :], reads=[], writes=[x1t_b])
                S.op("sync", "dma_start", out=x1t[:, D:2 * D], in_=x1_d[row + 128:row + 256, :], reads=[], writes=[x1t_b])
                S.op("sync", "dma_start", out=xAt[:, 0:D], in_=x_in[row:row + 128, :], reads=[], writes=[xAt_b])
                S.op("sync", "dma_start", out=xAt[:, D:2 * D], in_=x_in[row + 128:row + 256, :], reads=[], writes=[xAt_b])
                for sub in range(2):
                    ln(x1t[:, sub * D:(sub + 1) * D], x1t_b, h2[:], h2_b, xc[:], xc_b, sq[:], sq_b, stt[:], stt_b,
                       gain=G2[:], gain_buf=G2_b, bias=B2[:], bias_buf=B2_b)
                    for fc in range(8):
                        tps(h2[:, fc * 128:(fc + 1) * 128], h2_b, h2FM[:, fc * 256 + sub * 128:fc * 256 + (sub + 1) * 128],
                            h2FM_b, TP, TP_b, fc, ident_t[:], ident_b)
                for oc in range(32):
                    for fc in range(8):
                        S.op("tensor", "matmul", M1[:], W1[:, fc * 4096 + oc * 128:fc * 4096 + (oc + 1) * 128],
                             h2FM[:, fc * 256:(fc + 1) * 256], start=(fc == 0), stop=(fc == 7),
                             reads=[W1_b, h2FM_b], writes=[M1_b])
                    S.op("scalar", "activation", h3FM[:, oc * 256:(oc + 1) * 256], M1[:], AF.Gelu,
                         bias=B1a[:, oc:oc + 1], reads=[M1_b, B1a_b], writes=[h3FM_b])
                S.op("vector", "tensor_add", xAt[:], xAt[:], x1t[:], reads=[xAt_b, x1t_b], writes=[xAt_b])
                S.op("vector", "tensor_add", xAt[:, 0:D], xAt[:, 0:D], B2c[:], reads=[xAt_b, B2c_b], writes=[xAt_b])
                S.op("vector", "tensor_add", xAt[:, D:2 * D], xAt[:, D:2 * D], B2c[:], reads=[xAt_b, B2c_b], writes=[xAt_b])
                for sub in range(2):
                    for hf in range(2):
                        for fc in range(32):
                            S.op("tensor", "matmul", M2[:], h3FM[:, fc * 256 + sub * 128:fc * 256 + (sub + 1) * 128],
                                 W2[:, fc * 1024 + hf * 512:fc * 1024 + (hf + 1) * 512],
                                 start=(fc == 0), stop=(fc == 31), reads=[h3FM_b, W2_b], writes=[M2_b])
                        S.op("vector", "tensor_add", xAt[:, sub * D + hf * 512:sub * D + (hf + 1) * 512],
                             xAt[:, sub * D + hf * 512:sub * D + (hf + 1) * 512], M2[:],
                             reads=[xAt_b, M2_b], writes=[xAt_b])
                    ln(xAt[:, sub * D:(sub + 1) * D], xAt_b, xit[:, 0:D], xit_b,
                       xc[:], xc_b, sq[:], sq_b, stt[:], stt_b)
                    S.op("gpsimd", "dma_start", out=xA_d[row + sub * 128:row + sub * 128 + 128, :],
                         in_=xAt[:, sub * D:(sub + 1) * D], reads=[xAt_b], writes=[])
                    S.op("gpsimd", "dma_start", out=xi_d[row + sub * 128:row + sub * 128 + 128, :],
                         in_=xit[:, 0:D], reads=[xit_b], writes=[])
            S.barrier()
            S.replay(block, sems)

        def f_attn(l, in_d, out_d):
            W = fl[l]
            with ExitStack() as ph:
                sb = lambda n, shp, dt=F32: ph.enter_context(nc.sbuf_tensor(n + f"_a{l}", shp, dt))
                ps = lambda n, shp, dt=F32: ph.enter_context(nc.psum_tensor(n + f"_a{l}", shp, dt))
                FQ = sb("FQ", [128, 8 * 1536], BF16); FQ_b = Buf()
                FO = sb("FO", [128, 4 * 1024], BF16); FO_b = Buf()
                QS = sb("QS", [128, 512]); QS_b = Buf()
                KS = sb("KS", [128, 512]); KS_b = Buf()
                MSK = sb("MSKf", [128, 8 * 1024], BF16); MSK_b = Buf()
                with ExitStack() as wl:
                    stag = wl.enter_context(nc.sbuf_tensor(f"stg3_{l}", [128, 1536], F32)); stag_b = Buf()
                    load_w_bf16(W["qkv"], 8, 1536, lambda pc: FQ[:, pc * 1536:(pc + 1) * 1536], stag, stag_b, FQ_b)
                    load_w_bf16(W["wo"], 4, 1024, lambda pc: FO[:, pc * 1024:(pc + 1) * 1024], stag, stag_b, FO_b)
                    for i in range(8):
                        S.op("sync", "dma_start", out=MSK[:, i * 1024:(i + 1) * 1024], in_=maskF[i, :, :],
                             reads=[], writes=[MSK_b])
                    S.op("sync", "dma_start", out=QS[:], in_=W["qs"][:, :], reads=[], writes=[QS_b])
                    S.op("sync", "dma_start", out=KS[:], in_=W["ks"][:, :], reads=[], writes=[KS_b])
                    S.barrier()
                xch = sb("xch", [128, 4 * D]); xch_b = Buf()
                xsh = sb("xsh", [128, 512]); xsh_b = Buf()
                hbf = sb("hbf", [128, D], BF16); hbf_b = Buf()
                shbf = sb("shbf", [128, 512], BF16); shbf_b = Buf()
                hFM = sb("hFM3", [128, 8 * 512], BF16); hFM_b = Buf()
                qtm = sb("qtm", [128, 512]); qtm_b = Buf()
                ktm = sb("ktm", [128, 512]); ktm_b = Buf()
                n2 = sb("n2", [128, 8]); n2_b = Buf()
                n2b = sb("n2b", [128, 8]); n2b_b = Buf()
                rot = sb("rot", [128, 512]); rot_b = Buf()
                tb1 = sb("tb1", [128, 512]); tb1_b = Buf()
                tb2 = sb("tb2", [128, 512]); tb2_b = Buf()
                rc = sb("rc", [128, 512]); rc_b = Buf()
                rs_ = sb("rs_", [128, 512]); rs_b = Buf()
                qbf = sb("qbf", [128, 512], BF16); qbf_b = Buf()
                kbf = sb("kbf", [128, 512], BF16); kbf_b = Buf()
                qFM = sb("qFMf", [128, 4 * 512], BF16); qFM_b = Buf()
                kFM = sb("kFMf", [128, 2 * 4 * 512], BF16); kFM_b = Buf()
                vTM = sb("vTMf", [128, 2 * 4 * 512], BF16); vTM_b = Buf()
                sm = sb("smf", [128, 1024]); sm_b = Buf()
                ebf = sb("ebff", [128, 1024], BF16); ebf_b = Buf()
                aT = sb("aTf", [128, 1024], BF16); aT_b = Buf()
                ao = sb("aof", [128, 4 * 512], BF16); ao_b = Buf()
                aoFM = sb("aoFMf", [128, 4 * 512], BF16); aoFM_b = Buf()
                yt = sb("ytf", [128, D]); yt_b = Buf()
                stt = sb("stt3", [128, 4]); stt_b = Buf()
                MM = ps("MM3", [128, 512], F32); MM_b = Buf()
                SS = ps("SS3", [128, 1024], F32); SS_b = Buf()
                OO = ps("OO3", [128, 512], F32); OO_b = Buf()
                TP = [ps("TP03", [128, 128], BF16), ps("TP13", [128, 128], BF16)]
                TP_b = [Buf(), Buf()]
                S.op("gpsimd", "memset", kFM[:], 0.0, reads=[], writes=[kFM_b])
                S.op("gpsimd", "memset", vTM[:], 0.0, reads=[], writes=[vTM_b])
                for c in range(NCH):
                    slot = c % 2
                    pslot = (c - 1) % 2
                    for tt in range(4):
                        row = c * 512 + tt * 128
                        S.op("sync", "dma_start", out=xch[:, tt * D:(tt + 1) * D], in_=in_d[row:row + 128, :],
                             reads=[], writes=[xch_b])
                        if row == 0:
                            S.op("gpsimd", "memset", xsh[0:1, :], 0.0, reads=[], writes=[xsh_b])
                            S.op("sync", "dma_start", out=xsh[1:128, :], in_=in_d[0:127, 512:1024],
                                 reads=[], writes=[xsh_b], deps=[])
                        else:
                            S.op("sync", "dma_start", out=xsh[:], in_=in_d[row - 1:row + 127, 512:1024],
                                 reads=[], writes=[xsh_b])
                        S.op("scalar", "copy", hbf[:], xch[:, tt * D:(tt + 1) * D], reads=[xch_b], writes=[hbf_b])
                        S.op("scalar", "copy", shbf[:], xsh[:], reads=[xsh_b], writes=[shbf_b])
                        for fc in range(4):
                            tps(hbf[:, fc * 128:(fc + 1) * 128], hbf_b, hFM[:, fc * 512 + tt * 128:fc * 512 + (tt + 1) * 128],
                                hFM_b, TP, TP_b, fc, ident_t[:], ident_b)
                        for fc in range(4):
                            tps(shbf[:, fc * 128:(fc + 1) * 128], shbf_b,
                                hFM[:, (4 + fc) * 512 + tt * 128:(4 + fc) * 512 + (tt + 1) * 128],
                                hFM_b, TP, TP_b, fc, ident_t[:], ident_b)
                    for tt in range(4):
                        row = c * 512 + tt * 128
                        for oq in range(3):
                            for fc in range(8):
                                S.op("tensor", "matmul", MM[:], hFM[:, fc * 512 + tt * 128:fc * 512 + (tt + 1) * 128],
                                     FQ[:, fc * 1536 + oq * 512:fc * 1536 + (oq + 1) * 512],
                                     start=(fc == 0), stop=(fc == 7), reads=[hFM_b, FQ_b], writes=[MM_b])
                            if oq == 0:
                                S.op("vector", "tensor_copy", qtm[:], MM[:], reads=[MM_b], writes=[qtm_b])
                            elif oq == 1:
                                S.op("vector", "tensor_copy", ktm[:], MM[:], reads=[MM_b], writes=[ktm_b])
                            else:
                                S.op("scalar", "copy", vTM[:, slot * 2048 + tt * 512:slot * 2048 + (tt + 1) * 512],
                                     MM[:], reads=[MM_b], writes=[vTM_b])
                        for (ttm, ttm_b, tbl_c, tbl_s, scl, scl_b, obf, obf_b) in (
                                (qtm, qtm_b, rqc, rqs, QS, QS_b, qbf, qbf_b),
                                (ktm, ktm_b, rkc, rks, KS, KS_b, kbf, kbf_b)):
                            S.op("vector", "tensor_mul", tb1[:], ttm[:], ttm[:], reads=[ttm_b], writes=[tb1_b])
                            for h8 in range(8):
                                S.op("vector", "reduce_sum", n2[:, h8:h8 + 1], tb1[:, h8 * 64:(h8 + 1) * 64], AX,
                                     reads=[tb1_b], writes=[n2_b])
                            S.op("scalar", "activation", n2b[:], n2[:], AF.Sqrt, bias=CONST[:, 1:2],
                                 reads=[n2_b, CONST_b], writes=[n2b_b])
                            S.op("vector", "reciprocal", n2[:], n2b[:], reads=[n2b_b], writes=[n2_b])
                            for h8 in range(8):
                                S.op("vector", "tensor_scalar_mul", ttm[:, h8 * 64:(h8 + 1) * 64],
                                     ttm[:, h8 * 64:(h8 + 1) * 64], n2[:, h8:h8 + 1],
                                     reads=[ttm_b, n2_b], writes=[ttm_b])
                            S.op("vector", "tensor_mul", ttm[:], ttm[:], scl[:], reads=[ttm_b, scl_b], writes=[ttm_b])
                            S.op("scalar", "mul", rot[:].rearrange("p (h t f) -> p h t f", h=8, t=2)[:, :, 0, :],
                                 ttm[:].rearrange("p (h t f) -> p h t f", h=8, t=2)[:, :, 1, :], -1.0,
                                 reads=[ttm_b], writes=[rot_b])
                            S.op("scalar", "copy", rot[:].rearrange("p (h t f) -> p h t f", h=8, t=2)[:, :, 1, :],
                                 ttm[:].rearrange("p (h t f) -> p h t f", h=8, t=2)[:, :, 0, :],
                                 reads=[ttm_b], writes=[rot_b])
                            S.op("sync", "dma_start", out=rc[:], in_=tbl_c[row:row + 128, :], reads=[], writes=[rc_b])
                            S.op("sync", "dma_start", out=rs_[:], in_=tbl_s[row:row + 128, :], reads=[], writes=[rs_b])
                            S.op("vector", "tensor_mul", tb1[:], ttm[:], rc[:], reads=[ttm_b, rc_b], writes=[tb1_b])
                            S.op("vector", "tensor_mul", tb2[:], rot[:], rs_[:], reads=[rot_b, rs_b], writes=[tb2_b])
                            S.op("vector", "tensor_add", obf[:], tb1[:], tb2[:], reads=[tb1_b, tb2_b], writes=[obf_b])
                        for h8 in range(8):
                            tps(qbf[:, h8 * 64:(h8 + 1) * 64], qbf_b,
                                qFM[(h8 % 2) * 64:(h8 % 2) * 64 + 64, (h8 // 2) * 512 + tt * 128:(h8 // 2) * 512 + (tt + 1) * 128],
                                qFM_b, TP, TP_b, h8, ident_t[:], ident_b)
                            tps(kbf[:, h8 * 64:(h8 + 1) * 64], kbf_b,
                                kFM[(h8 % 2) * 64:(h8 % 2) * 64 + 64, slot * 2048 + (h8 // 2) * 512 + tt * 128:slot * 2048 + (h8 // 2) * 512 + (tt + 1) * 128],
                                kFM_b, TP, TP_b, h8 + 1, ident_t[:], ident_b)
                    mset = 4 if c == 0 else 0
                    for h in range(8):
                        hp = h // 2
                        po = (h % 2) * 64
                        for r in range(4):
                            S.op("tensor", "matmul", SS[:, 0:512], qFM[po:po + 64, hp * 512 + r * 128:hp * 512 + (r + 1) * 128],
                                 kFM[po:po + 64, pslot * 2048 + hp * 512:pslot * 2048 + (hp + 1) * 512],
                                 start=True, stop=True, reads=[qFM_b, kFM_b], writes=[SS_b])
                            S.op("tensor", "matmul", SS[:, 512:1024], qFM[po:po + 64, hp * 512 + r * 128:hp * 512 + (r + 1) * 128],
                                 kFM[po:po + 64, slot * 2048 + hp * 512:slot * 2048 + (hp + 1) * 512],
                                 start=True, stop=True, reads=[qFM_b, kFM_b], writes=[SS_b])
                            mc = (mset + r) * 1024
                            S.op("vector", "tensor_add", sm[:, 0:512], SS[:, 0:512], MSK[:, mc:mc + 512],
                                 reads=[SS_b, MSK_b], writes=[sm_b])
                            S.op("vector", "tensor_add", sm[:, 512:1024], SS[:, 512:1024], MSK[:, mc + 512:mc + 1024],
                                 reads=[SS_b, MSK_b], writes=[sm_b])
                            S.op("vector", "reduce_max", stt[:, 0:1], sm[:], AX, reads=[sm_b], writes=[stt_b])
                            S.op("scalar", "mul", stt[:, 1:2], stt[:, 0:1], -1.0, reads=[stt_b], writes=[stt_b])
                            S.op("scalar", "activation", ebf[:], sm[:], AF.Exp, bias=stt[:, 1:2],
                                 reads=[sm_b, stt_b], writes=[ebf_b])
                            S.op("vector", "reduce_sum", stt[:, 2:3], ebf[:], AX, reads=[ebf_b], writes=[stt_b])
                            S.op("vector", "reciprocal", stt[:, 3:4], stt[:, 2:3], reads=[stt_b], writes=[stt_b])
                            for kk in range(8):
                                tps(ebf[:, kk * 128:(kk + 1) * 128], ebf_b, aT[:, kk * 128:(kk + 1) * 128], aT_b,
                                    TP, TP_b, kk, ident_t[:], ident_b)
                            for kk in range(8):
                                vcol = (pslot if kk < 4 else slot) * 2048 + (kk % 4) * 512 + h * 64
                                S.op("tensor", "matmul", OO[:, 0:64], aT[:, kk * 128:(kk + 1) * 128],
                                     vTM[:, vcol:vcol + 64], start=(kk == 0), stop=(kk == 7),
                                     reads=[aT_b, vTM_b], writes=[OO_b])
                            S.op("vector", "tensor_scalar_mul", ao[:, r * 512 + h * 64:r * 512 + (h + 1) * 64],
                                 OO[:, 0:64], stt[:, 3:4], reads=[OO_b, stt_b], writes=[ao_b])
                    for tt in range(4):
                        for fc in range(4):
                            tps(ao[:, tt * 512 + fc * 128:tt * 512 + (fc + 1) * 128], ao_b,
                                aoFM[:, fc * 512 + tt * 128:fc * 512 + (tt + 1) * 128], aoFM_b,
                                TP, TP_b, fc, ident_t[:], ident_b)
                    for tt in range(4):
                        for hf in range(2):
                            for fc in range(4):
                                S.op("tensor", "matmul", MM[:], aoFM[:, fc * 512 + tt * 128:fc * 512 + (tt + 1) * 128],
                                     FO[:, fc * 1024 + hf * 512:fc * 1024 + (hf + 1) * 512],
                                     start=(fc == 0), stop=(fc == 3), reads=[aoFM_b, FO_b], writes=[MM_b])
                            S.op("vector", "tensor_add", yt[:, hf * 512:(hf + 1) * 512],
                                 xch[:, tt * 1024 + hf * 512:tt * 1024 + (hf + 1) * 512], MM[:],
                                 reads=[xch_b, MM_b], writes=[yt_b])
                        row = c * 512 + tt * 128
                        S.op("gpsimd", "dma_start", out=out_d[row:row + 128, :], in_=yt[:], reads=[yt_b], writes=[])
                S.barrier()
                S.replay(block, sems)

        def f_mlp1(l, in_d):
            W = fl[l]
            with ExitStack() as ph:
                sb = lambda n, shp, dt=F32: ph.enter_context(nc.sbuf_tensor(n + f"_m{l}", shp, dt))
                ps = lambda n, shp, dt=F32: ph.enter_context(nc.psum_tensor(n + f"_m{l}", shp, dt))
                WA = sb("WA", [128, 8 * 2816], BF16); WA_b = Buf()
                WG = sb("WG", [128, 8 * 2816], BF16); WG_b = Buf()
                LG = sb("LG", [128, D]); LG_b = Buf()
                LB = sb("LB", [128, D]); LB_b = Buf()
                BA = sb("BA", [128, 22]); BA_b = Buf()
                BG = sb("BG", [128, 22]); BG_b = Buf()
                with ExitStack() as wl:
                    stag = wl.enter_context(nc.sbuf_tensor(f"stg4_{l}", [128, 2816], F32)); stag_b = Buf()
                    load_w_bf16(W["w1a"], 8, 2816, lambda pc: WA[:, pc * 2816:(pc + 1) * 2816], stag, stag_b, WA_b)
                    load_w_bf16(W["w1g"], 8, 2816, lambda pc: WG[:, pc * 2816:(pc + 1) * 2816], stag, stag_b, WG_b)
                    S.op("sync", "dma_start", out=LG[:], in_=W["lg"][:, :], reads=[], writes=[LG_b])
                    S.op("sync", "dma_start", out=LB[:], in_=W["lb"][:, :], reads=[], writes=[LB_b])
                    S.op("sync", "dma_start", out=BA[:], in_=W["b1a"][:, :], reads=[], writes=[BA_b])
                    S.op("sync", "dma_start", out=BG[:], in_=W["b1g"][:, :], reads=[], writes=[BG_b])
                    S.barrier()
                xt = sb("xt4", [128, D]); xt_b = Buf()
                xsh = sb("xsh4", [128, 512]); xsh_b = Buf()
                hh = sb("hh4", [128, D]); hh_b = Buf()
                xc = sb("xc4", [128, D]); xc_b = Buf()
                sq = sb("sq4", [128, D]); sq_b = Buf()
                stt = sb("stt4", [128, 4]); stt_b = Buf()
                hb = sb("hb4", [128, D], BF16); hb_b = Buf()
                hFM = sb("hFM4", [128, 8 * 512], BF16); hFM_b = Buf()
                ga = sb("ga4", [128, 512], BF16); ga_b = Buf()
                ab = sb("ab4", [128, 512]); ab_b = Buf()
                mFM = sb("mFM4", [128, 22 * 512], BF16); mFM_b = Buf()
                A1 = ps("A1", [128, 512], F32); A1_b = Buf()
                G1p = ps("G1p", [128, 512], F32); G1p_b = Buf()
                TP = [ps("TP04", [128, 128], BF16), ps("TP14", [128, 128], BF16)]
                TP_b = [Buf(), Buf()]
                for it in range(NCH):
                    for tt in range(4):
                        row = it * 512 + tt * 128
                        S.op("sync", "dma_start", out=xt[:], in_=in_d[row:row + 128, :], reads=[], writes=[xt_b])
                        if row == 0:
                            S.op("gpsimd", "memset", xsh[0:1, :], 0.0, reads=[], writes=[xsh_b])
                            S.op("sync", "dma_start", out=xsh[1:128, :], in_=in_d[0:127, 512:1024],
                                 reads=[], writes=[xsh_b])
                        else:
                            S.op("sync", "dma_start", out=xsh[:], in_=in_d[row - 1:row + 127, 512:1024],
                                 reads=[], writes=[xsh_b])
                        S.op("vector", "tensor_copy", hh[:, 0:512], xt[:, 0:512], reads=[xt_b], writes=[hh_b])
                        S.op("vector", "tensor_copy", hh[:, 512:1024], xsh[:], reads=[xsh_b], writes=[hh_b])
                        ln(hh[:], hh_b, hb[:], hb_b, xc[:], xc_b, sq[:], sq_b, stt[:], stt_b,
                           gain=LG[:], gain_buf=LG_b, bias=LB[:], bias_buf=LB_b)
                        for fc in range(8):
                            tps(hb[:, fc * 128:(fc + 1) * 128], hb_b, hFM[:, fc * 512 + tt * 128:fc * 512 + (tt + 1) * 128],
                                hFM_b, TP, TP_b, fc, ident_t[:], ident_b)
                    for oc in range(22):
                        for fc in range(8):
                            S.op("tensor", "matmul", A1[:], WA[:, fc * 2816 + oc * 128:fc * 2816 + (oc + 1) * 128],
                                 hFM[:, fc * 512:(fc + 1) * 512], start=(fc == 0), stop=(fc == 7),
                                 reads=[WA_b, hFM_b], writes=[A1_b])
                        for fc in range(8):
                            S.op("tensor", "matmul", G1p[:], WG[:, fc * 2816 + oc * 128:fc * 2816 + (oc + 1) * 128],
                                 hFM[:, fc * 512:(fc + 1) * 512], start=(fc == 0), stop=(fc == 7),
                                 reads=[WG_b, hFM_b], writes=[G1p_b])
                        S.op("scalar", "activation", ga[:], G1p[:], AF.Gelu, bias=BG[:, oc:oc + 1],
                             reads=[G1p_b, BG_b], writes=[ga_b])
                        S.op("vector", "tensor_scalar_add", ab[:], A1[:], BA[:, oc:oc + 1],
                             reads=[A1_b, BA_b], writes=[ab_b])
                        S.op("vector", "tensor_mul", mFM[:, oc * 512:(oc + 1) * 512], ab[:], ga[:],
                             reads=[ab_b, ga_b], writes=[mFM_b])
                    S.op("gpsimd", "dma_start", out=m_d[it, :, :], in_=mFM[:], reads=[mFM_b], writes=[])
                S.barrier()
                S.replay(block, sems)

        def f_mlp2(l, res_d, out_d, add_d=None):
            W = fl[l]
            with ExitStack() as ph:
                sb = lambda n, shp, dt=F32: ph.enter_context(nc.sbuf_tensor(n + f"_n{l}", shp, dt))
                ps = lambda n, shp, dt=F32: ph.enter_context(nc.psum_tensor(n + f"_n{l}", shp, dt))
                W2f = sb("W2f", [128, 22 * 1024], BF16); W2f_b = Buf()
                B2f = sb("B2f", [128, D]); B2f_b = Buf()
                with ExitStack() as wl:
                    stag = wl.enter_context(nc.sbuf_tensor(f"stg5_{l}", [128, 1024], F32)); stag_b = Buf()
                    load_w_bf16(W["w2"], 22, 1024, lambda pc: W2f[:, pc * 1024:(pc + 1) * 1024], stag, stag_b, W2f_b)
                    S.op("sync", "dma_start", out=B2f[:], in_=W["b2"][:, :], reads=[], writes=[B2f_b])
                    S.barrier()
                mFM = sb("mFM5", [128, 22 * 512], BF16); mFM_b = Buf()
                xt = sb("xt5", [128, 4 * D]); xt_b = Buf()
                xadd = sb("xadd5", [128, D]); xadd_b = Buf()
                M2 = ps("M25", [128, 512], F32); M2_b = Buf()
                for it in range(NCH):
                    S.op("sync", "dma_start", out=mFM[:], in_=m_d[it, :, :], reads=[], writes=[mFM_b])
                    for tt in range(4):
                        row = it * 512 + tt * 128
                        S.op("sync", "dma_start", out=xt[:, tt * D:(tt + 1) * D], in_=res_d[row:row + 128, :],
                             reads=[], writes=[xt_b])
                        S.op("vector", "tensor_add", xt[:, tt * D:(tt + 1) * D], xt[:, tt * D:(tt + 1) * D], B2f[:],
                             reads=[xt_b, B2f_b], writes=[xt_b])
                        if add_d is not None:
                            S.op("sync", "dma_start", out=xadd[:, 0:D], in_=add_d[row:row + 128, :],
                                 reads=[], writes=[xadd_b])
                            S.op("vector", "tensor_add", xt[:, tt * D:(tt + 1) * D], xt[:, tt * D:(tt + 1) * D],
                                 xadd[:, 0:D], reads=[xt_b, xadd_b], writes=[xt_b])
                    for tt in range(4):
                        row = it * 512 + tt * 128
                        for hf in range(2):
                            for fc in range(22):
                                S.op("tensor", "matmul", M2[:], mFM[:, fc * 512 + tt * 128:fc * 512 + (tt + 1) * 128],
                                     W2f[:, fc * 1024 + hf * 512:fc * 1024 + (hf + 1) * 512],
                                     start=(fc == 0), stop=(fc == 21), reads=[mFM_b, W2f_b], writes=[M2_b])
                            S.op("vector", "tensor_add", xt[:, tt * D + hf * 512:tt * D + (hf + 1) * 512],
                                 xt[:, tt * D + hf * 512:tt * D + (hf + 1) * 512], M2[:],
                                 reads=[xt_b, M2_b], writes=[xt_b])
                        S.op("gpsimd", "dma_start", out=out_d[row:row + 128, :], in_=xt[:, tt * D:(tt + 1) * D],
                             reads=[xt_b], writes=[])
                S.barrier()
                S.replay(block, sems)

        f_attn(0, xi_d, ya_d)
        f_mlp1(0, ya_d)
        f_mlp2(0, ya_d, z_d)
        f_attn(1, z_d, ya_d)
        f_mlp1(1, ya_d)
        f_mlp2(1, ya_d, y_out, add_d=xA_d)

    es.close()
    return nc


def kernel(**inp):
    x = np.asarray(inp["L0_attn_x"], np.float32)

    def bc(v):
        return np.ascontiguousarray(np.broadcast_to(np.asarray(v, np.float32), (128, v.shape[-1])))

    wqkvT = np.ascontiguousarray(np.asarray(inp["L0_attn_w_qkv"], np.float32).T)
    woT = np.ascontiguousarray(np.asarray(inp["L0_attn_w_o"], np.float32).T)
    w1T = np.ascontiguousarray(np.asarray(inp["L0_attn_mlp_w1"], np.float32).T)
    w2T = np.ascontiguousarray(np.asarray(inp["L0_attn_mlp_w2"], np.float32).T)
    b1A = np.ascontiguousarray(np.asarray(inp["L0_attn_mlp_b1"], np.float32).reshape(32, 128).T)

    ii = np.arange(512)[:, None]
    jj = np.arange(1024)[None, :]
    bandA = (jj >= ii + 1) & (jj <= ii + 512)
    mA = np.where(bandA, 0.0, NEG).astype(np.float32)
    mA0 = np.where(bandA & (jj >= 512), 0.0, NEG).astype(np.float32)
    bandF = (jj >= ii) & (jj <= ii + 512)
    mF = np.where(bandF, 0.0, NEG).astype(np.float32)
    mF0 = np.where(bandF & (jj >= 512), 0.0, NEG).astype(np.float32)
    maskA = np.concatenate([mA.reshape(4, 128, 1024), mA0.reshape(4, 128, 1024)], 0).astype(ml_dtypes.bfloat16)
    maskF = np.concatenate([mF.reshape(4, 128, 1024), mF0.reshape(4, 128, 1024)], 0).astype(ml_dtypes.bfloat16)
    identH = np.eye(128, dtype=ml_dtypes.bfloat16)

    n = 16384
    dh = 64
    half = np.arange(0, dh, 2, dtype=np.float64)
    inv_freq = 1.0 / 10000.0 ** (half / dh)
    t = np.arange(n, dtype=np.float64)
    fr = t[:, None] * inv_freq[None, :]
    fr = np.concatenate([fr, fr], -1)
    base = (half + 0.4 * dh) / (1.4 * dh)
    power = (t - n // 2) / float(256)
    sc = base[None, :] ** power[:, None]
    sc = np.concatenate([sc, sc], -1)
    cos = np.cos(fr); sin = np.sin(fr)
    qc = (cos * sc * 8.0).astype(np.float32)
    qs = (sin * sc * 8.0).astype(np.float32)
    kc = (cos / sc).astype(np.float32)
    ks = (sin / sc).astype(np.float32)
    tile8 = lambda a: np.ascontiguousarray(np.tile(a, (1, 8)))
    qc, qs, kc, ks = tile8(qc), tile8(qs), tile8(kc), tile8(ks)

    common = dict(
        wqkvT=wqkvT, woT=woT,
        g1b=bc(np.asarray(inp["L0_attn_ln1_g"], np.float32)), b1b=bc(np.asarray(inp["L0_attn_ln1_b"], np.float32)),
        g2b=bc(np.asarray(inp["L0_attn_ln2_g"], np.float32)), b2b=bc(np.asarray(inp["L0_attn_ln2_b"], np.float32)),
        w1T=w1T, b1A=b1A, w2T=w2T, b2bc=bc(np.asarray(inp["L0_attn_mlp_b2"], np.float32)),
        maskA=maskA, maskF=maskF, identH=identH,
    )
    fw1 = np.asarray(inp["L0_ffn_w1"], np.float32)
    fw2 = np.asarray(inp["L0_ffn_w2"], np.float32)
    for l in range(2):
        w1a = np.zeros((1024, 2816), np.float32); w1a[:, :2730] = fw1[l][:, :2730]
        w1g = np.zeros((1024, 2816), np.float32); w1g[:, :2730] = fw1[l][:, 2730:]
        b1 = np.asarray(inp["L0_ffn_b1"], np.float32)[l]
        b1a = np.zeros(2816, np.float32); b1a[:2730] = b1[:2730]
        b1g = np.zeros(2816, np.float32); b1g[:2730] = b1[2730:]
        w2p = np.zeros((2816, 1024), np.float32); w2p[:2730] = fw2[l]
        common.update({
            f"fqkv{l}": np.ascontiguousarray(np.asarray(inp["L0_ffn_wqkv"], np.float32)[l]),
            f"fqs{l}": bc(np.tile(np.asarray(inp["L0_ffn_q_scale"], np.float32)[l], 8)),
            f"fks{l}": bc(np.tile(np.asarray(inp["L0_ffn_k_scale"], np.float32)[l], 8)),
            f"fwo{l}": np.ascontiguousarray(np.asarray(inp["L0_ffn_wo"], np.float32)[l]),
            f"flg{l}": bc(np.asarray(inp["L0_ffn_ln_g"], np.float32)[l]),
            f"flb{l}": bc(np.asarray(inp["L0_ffn_ln_b"], np.float32)[l]),
            f"fw1a{l}": w1a, f"fw1g{l}": w1g,
            f"fb1a{l}": np.ascontiguousarray(b1a.reshape(22, 128).T),
            f"fb1g{l}": np.ascontiguousarray(b1g.reshape(22, 128).T),
            f"fw2{l}": w2p,
            f"fb2bc{l}": bc(np.asarray(inp["L0_ffn_b2"], np.float32)[l]),
        })

    out_start = [0, 5632, 9216, 12800]
    comp_start = [0, 3584, 7168, 10752]
    in_maps = []
    for core in range(8):
        b, q = core // 4, core % 4
        m = dict(common)
        cs = comp_start[q]
        m["x"] = np.ascontiguousarray(x[b, cs:cs + T, :])
        gs = cs
        m["rqc"] = np.ascontiguousarray(qc[gs:gs + T])
        m["rqs"] = np.ascontiguousarray(qs[gs:gs + T])
        m["rkc"] = np.ascontiguousarray(kc[gs:gs + T])
        m["rks"] = np.ascontiguousarray(ks[gs:gs + T])
        in_maps.append(m)

    nc = build_program()
    res = run_bass_kernel_spmd(nc, in_maps, core_ids=list(range(8)))
    out = np.empty((2, 16384, 1024), np.float32)
    for core in range(8):
        b, q = core // 4, core % 4
        yc = np.asarray(res.results[core]["y"])
        skip = 0 if q == 0 else 2048
        out[b, out_start[q]:out_start[q] + (T - skip), :] = yc[skip:]
    return out


# version 03 (final)
# speedup vs baseline: 1.0078x; 1.0078x over previous
import numpy as np
import ml_dtypes
from contextlib import ExitStack
import concourse.bass as bass
import concourse.mybir as mybir
from concourse.bass_utils import run_bass_kernel_spmd

F32 = mybir.dt.float32
BF16 = mybir.dt.bfloat16
AX = mybir.AxisListType.X
AF = mybir.ActivationFunctionType
ALU = mybir.AluOpType

T = 5632
NTT = T // 128
NCH = T // 512
D = 1024
EPS = 1e-5
NEG = -1.0e30


class Buf:
    __slots__ = ("w", "r")

    def __init__(self):
        self.w = None
        self.r = []


class Sched:
    ENGS = ("sync", "gpsimd", "tensor", "vector", "scalar")

    def __init__(self):
        self.ops = {e: [] for e in self.ENGS}
        self.cnt = {e: 0 for e in self.ENGS}

    def op(self, eng, m, *args, reads=(), writes=(), deps=(), **kw):
        dd = []
        for b in reads:
            if b.w is not None:
                dd.append(b.w)
        for b in writes:
            dd += b.r
            if b.w is not None:
                dd.append(b.w)
        dd += list(deps)
        inc = 16 if m == "dma_start" else 1
        if m == "dma_start" and self.cnt[eng] > 0:
            dd.append((eng, self.cnt[eng]))
        self.cnt[eng] += inc
        tok = (eng, self.cnt[eng])
        waits = []
        for (pe, pv) in dd:
            if pe == eng and eng == "tensor":
                continue
            waits.append((pe, pv))
        self.ops[eng].append((waits, m, args, kw, inc))
        for b in reads:
            b.r.append(tok)
        for b in writes:
            b.w = tok
            b.r = []
        return tok

    def barrier(self):
        for e in self.ENGS:
            waits = [(p, self.cnt[p]) for p in self.ENGS if p != e]
            self.ops[e].append((waits, None, (), {}, 0))

    def replay(self, block, sems):
        for eng in self.ENGS:
            ops = self.ops[eng]
            if not ops:
                continue

            def mk(eng, ops):
                def f(proxy):
                    seen = {}
                    for (waits, m, args, kw, inc) in ops:
                        for (pe, pv) in waits:
                            if seen.get(pe, 0) >= pv:
                                continue
                            proxy.wait_ge(sems[pe], pv)
                            seen[pe] = pv
                        if m is None:
                            continue
                        r = getattr(proxy, m)(*args, **kw)
                        r.then_inc(sems[eng], inc)
                return f
            getattr(block, eng)(mk(eng, ops))
        self.ops = {e: [] for e in self.ENGS}


def build_program(debug=False):
    nc = bass.Bass("TRN2", target_bir_lowering=False)
    g = lambda n, shp, dt=F32: nc.dram_tensor(n, shp, dt, kind="ExternalInput").ap()

    x_in = g("x", [T, D])
    y_out = nc.dram_tensor("y", [T, D], F32, kind="ExternalOutput").ap()

    wqkvT = g("wqkvT", [D, 3072])
    woT = g("woT", [D, D])
    g1b = g("g1b", [128, D]); b1b = g("b1b", [128, D])
    g2b = g("g2b", [128, D]); b2b = g("b2b", [128, D])
    w1T = g("w1T", [D, 4096])
    b1A = g("b1A", [128, 32])
    w2T = g("w2T", [4096, D])
    b2bc = g("b2bc", [128, D])
    maskA = g("maskA", [8, 128, 1024], BF16)
    maskF = g("maskF", [8, 128, 1024], BF16)
    identH = g("identH", [128, 128], BF16)
    rqc = g("rqc", [T, 512]); rqs = g("rqs", [T, 512])
    rkc = g("rkc", [T, 512]); rks = g("rks", [T, 512])
    fl = []
    for l in range(2):
        fl.append(dict(
            qkv=g(f"fqkv{l}", [D, 1536]),
            qs=g(f"fqs{l}", [128, 512]), ks=g(f"fks{l}", [128, 512]),
            wo=g(f"fwo{l}", [512, D]),
            lg=g(f"flg{l}", [128, D]), lb=g(f"flb{l}", [128, D]),
            w1a=g(f"fw1a{l}", [D, 2816]), w1g=g(f"fw1g{l}", [D, 2816]),
            b1a=g(f"fb1a{l}", [128, 22]), b1g=g(f"fb1g{l}", [128, 22]),
            w2=g(f"fw2{l}", [2816, D]), b2=g(f"fb2bc{l}", [128, D]),
        ))

    ikind = "ExternalOutput" if debug else "Internal"
    dscr = lambda n, shp, dt=F32: nc.dram_tensor(n, shp, dt, kind="Internal").ap()
    dscr2 = lambda n, shp, dt=F32: nc.dram_tensor(n, shp, dt, kind=ikind).ap()
    x1_d = dscr2("x1_d", [T, D])
    xA_d = dscr2("xA_d", [T, D])
    xi_d = dscr2("xi_d", [T, D])
    ya_d = dscr("ya_d", [T, D])
    z_d = dscr2("z_d", [T, D])
    m_d = dscr("m_d", [NCH, 128, 22 * 512], BF16)

    S = Sched()
    es = ExitStack()
    sems = {e: es.enter_context(nc.semaphore(f"sem_{e}")) for e in Sched.ENGS}

    def ln(t_in, t_in_buf, out_ap, out_buf, xc_ap, xc_buf, sq_ap, sq_buf, st_ap, st_buf,
           gain=None, gain_buf=None, bias=None, bias_buf=None, width=D):
        S.op("vector", "reduce_sum", st_ap[:, 0:1], t_in, AX, reads=[t_in_buf], writes=[st_buf])
        S.op("scalar", "mul", st_ap[:, 1:2], st_ap[:, 0:1], 1.0 / width, reads=[st_buf], writes=[st_buf])
        S.op("vector", "tensor_scalar_sub", xc_ap, t_in, st_ap[:, 1:2], reads=[t_in_buf, st_buf], writes=[xc_buf])
        S.op("scalar", "square", sq_ap, xc_ap, reads=[xc_buf], writes=[sq_buf])
        S.op("vector", "reduce_sum", st_ap[:, 0:1], sq_ap, AX, reads=[sq_buf], writes=[st_buf])
        S.op("scalar", "activation", st_ap[:, 1:2], st_ap[:, 0:1], AF.Sqrt, bias=CONST[:, 0:1], scale=1.0 / width,
             reads=[st_buf, CONST_b], writes=[st_buf])
        S.op("vector", "reciprocal", st_ap[:, 2:3], st_ap[:, 1:2], reads=[st_buf], writes=[st_buf])
        if gain is None:
            S.op("vector", "tensor_scalar_mul", out_ap, xc_ap, st_ap[:, 2:3], reads=[xc_buf, st_buf], writes=[out_buf])
        else:
            S.op("vector", "tensor_scalar_mul", xc_ap, xc_ap, st_ap[:, 2:3], reads=[xc_buf, st_buf], writes=[xc_buf])
            S.op("vector", "tensor_mul", xc_ap, xc_ap, gain, reads=[xc_buf, gain_buf], writes=[xc_buf])
            S.op("vector", "tensor_add", out_ap, xc_ap, bias, reads=[xc_buf, bias_buf], writes=[out_buf])

    def load_w_bf16(src_ap, n_part, kcols, dst_ap_fn, stag, stag_buf, dst_buf, cap=4096):
        for pc in range(n_part):
            dst = dst_ap_fn(pc)
            for c0 in range(0, kcols, cap):
                c1 = min(kcols, c0 + cap)
                S.op("sync", "dma_start", out=stag[:, 0:c1 - c0], in_=src_ap[pc * 128:(pc + 1) * 128, c0:c1],
                     reads=[], writes=[stag_buf])
                S.op("scalar", "copy", dst[:, c0:c1], stag[:, 0:c1 - c0], reads=[stag_buf], writes=[dst_buf])

    def tps(in_ap, in_buf, out_ap, out_buf, tp, tp_bufs, idx, ident_ap, ident_buf):
        k = idx % 2
        fc = in_ap.shape[1]
        pc = in_ap.shape[0]
        S.op("tensor", "transpose", tp[k][0:fc, 0:pc], in_ap, ident_ap,
             reads=[in_buf, ident_buf], writes=[tp_bufs[k]])
        S.op("scalar", "copy", out_ap, tp[k][0:fc, 0:pc], reads=[tp_bufs[k]], writes=[out_buf])

    ident_t = es.enter_context(nc.sbuf_tensor("ident", [128, 128], BF16))
    ident_b = Buf()
    S.op("sync", "dma_start", out=ident_t[:], in_=identH[:, :], reads=[], writes=[ident_b])
    CONST = es.enter_context(nc.sbuf_tensor("CONST", [128, 2], F32))
    CONST_b = Buf()
    S.op("vector", "memset", CONST[:, 0:1], EPS, reads=[], writes=[CONST_b])
    S.op("vector", "memset", CONST[:, 1:2], 1e-24, reads=[], writes=[CONST_b])

    with nc.Block() as block:

        with ExitStack() as ph:
            sb = lambda n, shp, dt=F32: ph.enter_context(nc.sbuf_tensor(n, shp, dt))
            ps = lambda n, shp, dt=F32: ph.enter_context(nc.psum_tensor(n, shp, dt))
            WQ = sb("WQ", [128, 8 * 3072], BF16); WQ_b = Buf()
            WO = sb("WOp", [128, 8 * 1024], BF16); WO_b = Buf()
            MSK = sb("MSK", [128, 8 * 1024], BF16); MSK_b = Buf()
            G1 = sb("G1", [128, D]); G1_b = Buf()
            B1 = sb("B1", [128, D]); B1_b = Buf()
            with ExitStack() as wl:
                stag = wl.enter_context(nc.sbuf_tensor("stg1", [128, 3072], F32)); stag_b = Buf()
                load_w_bf16(wqkvT, 8, 3072, lambda pc: WQ[:, pc * 3072:(pc + 1) * 3072], stag, stag_b, WQ_b)
                load_w_bf16(woT, 8, 1024, lambda pc: WO[:, pc * 1024:(pc + 1) * 1024], stag, stag_b, WO_b)
                for i in range(8):
                    S.op("sync", "dma_start", out=MSK[:, i * 1024:(i + 1) * 1024], in_=maskA[i, :, :],
                         reads=[], writes=[MSK_b])
                S.op("sync", "dma_start", out=G1[:], in_=g1b[:, :], reads=[], writes=[G1_b])
                S.op("sync", "dma_start", out=B1[:], in_=b1b[:, :], reads=[], writes=[B1_b])
                S.barrier()

            xinner = sb("xinner", [128, 4 * D]); xinner_b = Buf()
            xc = sb("xc", [128, D]); xc_b = Buf()
            stt = sb("stt", [128, 4]); stt_b = Buf()
            h1 = sb("h1", [128, D], BF16); h1_b = Buf()
            h1FM = sb("h1FM", [128, 8 * 512], BF16); h1FM_b = Buf()
            qFM = sb("qFM", [128, 8 * 512], BF16); qFM_b = Buf()
            kFM = sb("kFM", [128, 2 * 8 * 512], BF16); kFM_b = Buf()
            vTM = sb("vTM", [128, 2 * 4 * 1024], BF16); vTM_b = Buf()
            sm = sb("sm", [128, 1024]); sm_b = Buf()
            ebf = sb("ebf", [128, 1024], BF16); ebf_b = Buf()
            aT = sb("aT", [128, 1024], BF16); aT_b = Buf()
            ao = sb("ao", [128, 4 * 1024], BF16); ao_b = Buf()
            aoFM = sb("aoFM", [128, 8 * 512], BF16); aoFM_b = Buf()
            x1sb = sb("x1sb", [128, D]); x1sb_b = Buf()
            MM = ps("MM", [128, 512], F32); MM_b = Buf()
            SS = ps("SS", [128, 1024], F32); SS_b = Buf()
            OO = ps("OO", [128, 512], F32); OO_b = Buf()
            TP = [ps("TP0", [128, 128], BF16), ps("TP1", [128, 128], BF16)]
            TP_b = [Buf(), Buf()]

            S.op("vector", "memset", kFM[:], 0.0, reads=[], writes=[kFM_b])
            S.op("vector", "memset", vTM[:], 0.0, reads=[], writes=[vTM_b])

            for c in range(NCH):
                slot = c % 2
                pslot = (c - 1) % 2
                for tt in range(4):
                    row = c * 512 + tt * 128
                    S.op("sync", "dma_start", out=xinner[:, tt * D:(tt + 1) * D], in_=x_in[row:row + 128, :],
                         reads=[], writes=[xinner_b])
                    ln(xinner[:, tt * D:(tt + 1) * D], xinner_b, xinner[:, tt * D:(tt + 1) * D], xinner_b,
                       xc[:], xc_b, sm[:], sm_b, stt[:], stt_b)
                    ln(xinner[:, tt * D:(tt + 1) * D], xinner_b, h1[:], h1_b, xc[:], xc_b, sm[:], sm_b, stt[:], stt_b,
                       gain=G1[:], gain_buf=G1_b, bias=B1[:], bias_buf=B1_b)
                    for fc in range(8):
                        tps(h1[:, fc * 128:(fc + 1) * 128], h1_b, h1FM[:, fc * 512 + tt * 128:fc * 512 + (tt + 1) * 128],
                            h1FM_b, TP, TP_b, fc, ident_t[:], ident_b)
                for oc in range(16):
                    dk = qFM if oc < 8 else kFM
                    dkb = qFM_b if oc < 8 else kFM_b
                    col = oc * 512 if oc < 8 else slot * 4096 + (oc - 8) * 512
                    for fc in range(8):
                        S.op("tensor", "matmul", MM[:], WQ[:, fc * 3072 + oc * 128:fc * 3072 + (oc + 1) * 128],
                             h1FM[:, fc * 512:(fc + 1) * 512], start=(fc == 0), stop=(fc == 7),
                             reads=[WQ_b, h1FM_b], writes=[MM_b])
                    if oc < 8:
                        S.op("scalar", "mul", dk[:, col:col + 512], MM[:], 0.125, reads=[MM_b], writes=[dkb])
                    else:
                        S.op("scalar", "copy", dk[:, col:col + 512], MM[:], reads=[MM_b], writes=[dkb])
                for tt in range(4):
                    for hf in range(2):
                        for fc in range(8):
                            S.op("tensor", "matmul", MM[:], h1FM[:, fc * 512 + tt * 128:fc * 512 + (tt + 1) * 128],
                                 WQ[:, fc * 3072 + 2048 + hf * 512:fc * 3072 + 2048 + (hf + 1) * 512],
                                 start=(fc == 0), stop=(fc == 7), reads=[h1FM_b, WQ_b], writes=[MM_b])
                        S.op("scalar", "copy", vTM[:, slot * 4096 + tt * 1024 + hf * 512: slot * 4096 + tt * 1024 + (hf + 1) * 512],
                             MM[:], reads=[MM_b], writes=[vTM_b])
                mset = 4 if c == 0 else 0
                for h in range(16):
                    hp = h // 2
                    po = (h % 2) * 64
                    for r in range(4):
                        S.op("tensor", "matmul", SS[:, 0:512], qFM[po:po + 64, hp * 512 + r * 128:hp * 512 + (r + 1) * 128],
                             kFM[po:po + 64, pslot * 4096 + hp * 512:pslot * 4096 + (hp + 1) * 512],
                             start=True, stop=True, reads=[qFM_b, kFM_b], writes=[SS_b])
                        S.op("tensor", "matmul", SS[:, 512:1024], qFM[po:po + 64, hp * 512 + r * 128:hp * 512 + (r + 1) * 128],
                             kFM[po:po + 64, slot * 4096 + hp * 512:slot * 4096 + (hp + 1) * 512],
                             start=True, stop=True, reads=[qFM_b, kFM_b], writes=[SS_b])
                        mc = (mset + r) * 1024
                        S.op("vector", "tensor_add", sm[:, 0:512], SS[:, 0:512], MSK[:, mc:mc + 512],
                             reads=[SS_b, MSK_b], writes=[sm_b])
                        S.op("vector", "tensor_add", sm[:, 512:1024], SS[:, 512:1024], MSK[:, mc + 512:mc + 1024],
                             reads=[SS_b, MSK_b], writes=[sm_b])
                        S.op("vector", "reduce_max", stt[:, 0:1], sm[:], AX, reads=[sm_b], writes=[stt_b])
                        S.op("scalar", "mul", stt[:, 1:2], stt[:, 0:1], -1.0, reads=[stt_b], writes=[stt_b])
                        S.op("scalar", "activation", ebf[:], sm[:], AF.Exp, bias=stt[:, 1:2],
                             reads=[sm_b, stt_b], writes=[ebf_b])
                        S.op("vector", "reduce_sum", stt[:, 2:3], ebf[:], AX, reads=[ebf_b], writes=[stt_b])
                        S.op("vector", "reciprocal", stt[:, 3:4], stt[:, 2:3], reads=[stt_b], writes=[stt_b])
                        for kk in range(8):
                            tps(ebf[:, kk * 128:(kk + 1) * 128], ebf_b, aT[:, kk * 128:(kk + 1) * 128], aT_b,
                                TP, TP_b, kk, ident_t[:], ident_b)
                        for kk in range(8):
                            vcol = (pslot if kk < 4 else slot) * 4096 + (kk % 4) * 1024 + h * 64
                            S.op("tensor", "matmul", OO[:, 0:64], aT[:, kk * 128:(kk + 1) * 128],
                                 vTM[:, vcol:vcol + 64], start=(kk == 0), stop=(kk == 7),
                                 reads=[aT_b, vTM_b], writes=[OO_b])
                        S.op("vector", "tensor_scalar_mul", ao[:, r * 1024 + h * 64:r * 1024 + (h + 1) * 64],
                             OO[:, 0:64], stt[:, 3:4], reads=[OO_b, stt_b], writes=[ao_b])
                for tt in range(4):
                    for fc in range(8):
                        tps(ao[:, tt * 1024 + fc * 128:tt * 1024 + (fc + 1) * 128], ao_b,
                            aoFM[:, fc * 512 + tt * 128:fc * 512 + (tt + 1) * 128], aoFM_b,
                            TP, TP_b, fc, ident_t[:], ident_b)
                for tt in range(4):
                    for hf in range(2):
                        for fc in range(8):
                            S.op("tensor", "matmul", MM[:], aoFM[:, fc * 512 + tt * 128:fc * 512 + (tt + 1) * 128],
                                 WO[:, fc * 1024 + hf * 512:fc * 1024 + (hf + 1) * 512],
                                 start=(fc == 0), stop=(fc == 7), reads=[aoFM_b, WO_b], writes=[MM_b])
                        S.op("vector", "tensor_add", x1sb[:, hf * 512:(hf + 1) * 512],
                             xinner[:, tt * 1024 + hf * 512:tt * 1024 + (hf + 1) * 512], MM[:],
                             reads=[xinner_b, MM_b], writes=[x1sb_b])
                    row = c * 512 + tt * 128
                    S.op("gpsimd", "dma_start", out=x1_d[row:row + 128, :], in_=x1sb[:], reads=[x1sb_b], writes=[])
            S.barrier()
            S.replay(block, sems)

        with ExitStack() as ph:
            sb = lambda n, shp, dt=F32: ph.enter_context(nc.sbuf_tensor(n, shp, dt))
            ps = lambda n, shp, dt=F32: ph.enter_context(nc.psum_tensor(n, shp, dt))
            W1 = sb("W1", [128, 8 * 4096], BF16); W1_b = Buf()
            W2 = sb("W2", [128, 32 * 1024], BF16); W2_b = Buf()
            G2 = sb("G2", [128, D]); G2_b = Buf()
            B2 = sb("B2", [128, D]); B2_b = Buf()
            B1a = sb("B1a", [128, 32]); B1a_b = Buf()
            B2c = sb("B2c", [128, D]); B2c_b = Buf()
            with ExitStack() as wl:
                stag = wl.enter_context(nc.sbuf_tensor("stg2", [128, 2048], F32)); stag_b = Buf()
                load_w_bf16(w1T, 8, 4096, lambda pc: W1[:, pc * 4096:(pc + 1) * 4096], stag, stag_b, W1_b, cap=2048)
                load_w_bf16(w2T, 32, 1024, lambda pc: W2[:, pc * 1024:(pc + 1) * 1024], stag, stag_b, W2_b, cap=2048)
                S.op("sync", "dma_start", out=G2[:], in_=g2b[:, :], reads=[], writes=[G2_b])
                S.op("sync", "dma_start", out=B2[:], in_=b2b[:, :], reads=[], writes=[B2_b])
                S.op("sync", "dma_start", out=B1a[:], in_=b1A[:, :], reads=[], writes=[B1a_b])
                S.op("sync", "dma_start", out=B2c[:], in_=b2bc[:, :], reads=[], writes=[B2c_b])
                S.barrier()
            x1t = sb("x1t", [128, 2 * D]); x1t_b = Buf()
            xAt = sb("xAt", [128, 2 * D]); xAt_b = Buf()
            xit = sb("xit", [128, D]); xit_b = Buf()
            xc = sb("xc2", [128, D]); xc_b = Buf()
            sq = sb("sq2", [128, D]); sq_b = Buf()
            stt = sb("stt2", [128, 4]); stt_b = Buf()
            h2 = sb("h2", [128, D], BF16); h2_b = Buf()
            h2FM = sb("h2FM", [128, 8 * 256], BF16); h2FM_b = Buf()
            h3FM = sb("h3FM", [128, 32 * 256], BF16); h3FM_b = Buf()
            M1 = ps("M1", [128, 256], F32); M1_b = Buf()
            M2 = ps("M2", [128, 512], F32); M2_b = Buf()
            TP = [ps("TP0b", [128, 128], BF16), ps("TP1b", [128, 128], BF16)]
            TP_b = [Buf(), Buf()]
            for it in range(T // 256):
                row = it * 256
                S.op("sync", "dma_start", out=x1t[:, 0:D], in_=x1_d[row:row + 128, :], reads=[], writes=[x1t_b])
                S.op("sync", "dma_start", out=x1t[:, D:2 * D], in_=x1_d[row + 128:row + 256, :], reads=[], writes=[x1t_b])
                S.op("sync", "dma_start", out=xAt[:, 0:D], in_=x_in[row:row + 128, :], reads=[], writes=[xAt_b])
                S.op("sync", "dma_start", out=xAt[:, D:2 * D], in_=x_in[row + 128:row + 256, :], reads=[], writes=[xAt_b])
                for sub in range(2):
                    ln(x1t[:, sub * D:(sub + 1) * D], x1t_b, h2[:], h2_b, xc[:], xc_b, sq[:], sq_b, stt[:], stt_b,
                       gain=G2[:], gain_buf=G2_b, bias=B2[:], bias_buf=B2_b)
                    for fc in range(8):
                        tps(h2[:, fc * 128:(fc + 1) * 128], h2_b, h2FM[:, fc * 256 + sub * 128:fc * 256 + (sub + 1) * 128],
                            h2FM_b, TP, TP_b, fc, ident_t[:], ident_b)
                for oc in range(32):
                    for fc in range(8):
                        S.op("tensor", "matmul", M1[:], W1[:, fc * 4096 + oc * 128:fc * 4096 + (oc + 1) * 128],
                             h2FM[:, fc * 256:(fc + 1) * 256], start=(fc == 0), stop=(fc == 7),
                             reads=[W1_b, h2FM_b], writes=[M1_b])
                    S.op("scalar", "activation", h3FM[:, oc * 256:(oc + 1) * 256], M1[:], AF.Gelu,
                         bias=B1a[:, oc:oc + 1], reads=[M1_b, B1a_b], writes=[h3FM_b])
                S.op("vector", "tensor_add", xAt[:], xAt[:], x1t[:], reads=[xAt_b, x1t_b], writes=[xAt_b])
                S.op("vector", "tensor_add", xAt[:, 0:D], xAt[:, 0:D], B2c[:], reads=[xAt_b, B2c_b], writes=[xAt_b])
                S.op("vector", "tensor_add", xAt[:, D:2 * D], xAt[:, D:2 * D], B2c[:], reads=[xAt_b, B2c_b], writes=[xAt_b])
                for sub in range(2):
                    for hf in range(2):
                        for fc in range(32):
                            S.op("tensor", "matmul", M2[:], h3FM[:, fc * 256 + sub * 128:fc * 256 + (sub + 1) * 128],
                                 W2[:, fc * 1024 + hf * 512:fc * 1024 + (hf + 1) * 512],
                                 start=(fc == 0), stop=(fc == 31), reads=[h3FM_b, W2_b], writes=[M2_b])
                        S.op("vector", "tensor_add", xAt[:, sub * D + hf * 512:sub * D + (hf + 1) * 512],
                             xAt[:, sub * D + hf * 512:sub * D + (hf + 1) * 512], M2[:],
                             reads=[xAt_b, M2_b], writes=[xAt_b])
                    ln(xAt[:, sub * D:(sub + 1) * D], xAt_b, xit[:, 0:D], xit_b,
                       xc[:], xc_b, sq[:], sq_b, stt[:], stt_b)
                    S.op("gpsimd", "dma_start", out=xA_d[row + sub * 128:row + sub * 128 + 128, :],
                         in_=xAt[:, sub * D:(sub + 1) * D], reads=[xAt_b], writes=[])
                    S.op("gpsimd", "dma_start", out=xi_d[row + sub * 128:row + sub * 128 + 128, :],
                         in_=xit[:, 0:D], reads=[xit_b], writes=[])
            S.barrier()
            S.replay(block, sems)

        def f_attn(l, in_d, out_d):
            W = fl[l]
            with ExitStack() as ph:
                sb = lambda n, shp, dt=F32: ph.enter_context(nc.sbuf_tensor(n + f"_a{l}", shp, dt))
                ps = lambda n, shp, dt=F32: ph.enter_context(nc.psum_tensor(n + f"_a{l}", shp, dt))
                FQ = sb("FQ", [128, 8 * 1536], BF16); FQ_b = Buf()
                FO = sb("FO", [128, 4 * 1024], BF16); FO_b = Buf()
                QS = sb("QS", [128, 512]); QS_b = Buf()
                KS = sb("KS", [128, 512]); KS_b = Buf()
                MSK = sb("MSKf", [128, 8 * 1024], BF16); MSK_b = Buf()
                with ExitStack() as wl:
                    stag = wl.enter_context(nc.sbuf_tensor(f"stg3_{l}", [128, 1536], F32)); stag_b = Buf()
                    load_w_bf16(W["qkv"], 8, 1536, lambda pc: FQ[:, pc * 1536:(pc + 1) * 1536], stag, stag_b, FQ_b)
                    load_w_bf16(W["wo"], 4, 1024, lambda pc: FO[:, pc * 1024:(pc + 1) * 1024], stag, stag_b, FO_b)
                    for i in range(8):
                        S.op("sync", "dma_start", out=MSK[:, i * 1024:(i + 1) * 1024], in_=maskF[i, :, :],
                             reads=[], writes=[MSK_b])
                    S.op("sync", "dma_start", out=QS[:], in_=W["qs"][:, :], reads=[], writes=[QS_b])
                    S.op("sync", "dma_start", out=KS[:], in_=W["ks"][:, :], reads=[], writes=[KS_b])
                    S.barrier()
                xch = sb("xch", [128, 4 * D]); xch_b = Buf()
                xsh = sb("xsh", [128, 512]); xsh_b = Buf()
                hbf = sb("hbf", [128, D], BF16); hbf_b = Buf()
                shbf = sb("shbf", [128, 512], BF16); shbf_b = Buf()
                hFM = sb("hFM3", [128, 8 * 512], BF16); hFM_b = Buf()
                qtm = sb("qtm", [128, 512]); qtm_b = Buf()
                ktm = sb("ktm", [128, 512]); ktm_b = Buf()
                n2 = sb("n2", [128, 8]); n2_b = Buf()
                n2b = sb("n2b", [128, 8]); n2b_b = Buf()
                rot = sb("rot", [128, 512]); rot_b = Buf()
                tb1 = sb("tb1", [128, 512]); tb1_b = Buf()
                tb2 = sb("tb2", [128, 512]); tb2_b = Buf()
                rc = sb("rc", [128, 512]); rc_b = Buf()
                rs_ = sb("rs_", [128, 512]); rs_b = Buf()
                qbf = sb("qbf", [128, 512], BF16); qbf_b = Buf()
                kbf = sb("kbf", [128, 512], BF16); kbf_b = Buf()
                qFM = sb("qFMf", [128, 4 * 512], BF16); qFM_b = Buf()
                kFM = sb("kFMf", [128, 2 * 4 * 512], BF16); kFM_b = Buf()
                vTM = sb("vTMf", [128, 2 * 4 * 512], BF16); vTM_b = Buf()
                sm = sb("smf", [128, 1024]); sm_b = Buf()
                ebf = sb("ebff", [128, 1024], BF16); ebf_b = Buf()
                aT = sb("aTf", [128, 1024], BF16); aT_b = Buf()
                ao = sb("aof", [128, 4 * 512], BF16); ao_b = Buf()
                aoFM = sb("aoFMf", [128, 4 * 512], BF16); aoFM_b = Buf()
                yt = sb("ytf", [128, D]); yt_b = Buf()
                stt = sb("stt3", [128, 4]); stt_b = Buf()
                MM = ps("MM3", [128, 512], F32); MM_b = Buf()
                SS = ps("SS3", [128, 1024], F32); SS_b = Buf()
                OO = ps("OO3", [128, 512], F32); OO_b = Buf()
                TP = [ps("TP03", [128, 128], BF16), ps("TP13", [128, 128], BF16)]
                TP_b = [Buf(), Buf()]
                S.op("vector", "memset", kFM[:], 0.0, reads=[], writes=[kFM_b])
                S.op("vector", "memset", vTM[:], 0.0, reads=[], writes=[vTM_b])
                for c in range(NCH):
                    slot = c % 2
                    pslot = (c - 1) % 2
                    for tt in range(4):
                        row = c * 512 + tt * 128
                        S.op("sync", "dma_start", out=xch[:, tt * D:(tt + 1) * D], in_=in_d[row:row + 128, :],
                             reads=[], writes=[xch_b])
                        if row == 0:
                            S.op("vector", "memset", xsh[0:1, :], 0.0, reads=[], writes=[xsh_b])
                            S.op("sync", "dma_start", out=xsh[1:128, :], in_=in_d[0:127, 512:1024],
                                 reads=[], writes=[xsh_b], deps=[])
                        else:
                            S.op("sync", "dma_start", out=xsh[:], in_=in_d[row - 1:row + 127, 512:1024],
                                 reads=[], writes=[xsh_b])
                        S.op("scalar", "copy", hbf[:], xch[:, tt * D:(tt + 1) * D], reads=[xch_b], writes=[hbf_b])
                        S.op("scalar", "copy", shbf[:], xsh[:], reads=[xsh_b], writes=[shbf_b])
                        for fc in range(4):
                            tps(hbf[:, fc * 128:(fc + 1) * 128], hbf_b, hFM[:, fc * 512 + tt * 128:fc * 512 + (tt + 1) * 128],
                                hFM_b, TP, TP_b, fc, ident_t[:], ident_b)
                        for fc in range(4):
                            tps(shbf[:, fc * 128:(fc + 1) * 128], shbf_b,
                                hFM[:, (4 + fc) * 512 + tt * 128:(4 + fc) * 512 + (tt + 1) * 128],
                                hFM_b, TP, TP_b, fc, ident_t[:], ident_b)
                    for tt in range(4):
                        row = c * 512 + tt * 128
                        for oq in range(3):
                            for fc in range(8):
                                S.op("tensor", "matmul", MM[:], hFM[:, fc * 512 + tt * 128:fc * 512 + (tt + 1) * 128],
                                     FQ[:, fc * 1536 + oq * 512:fc * 1536 + (oq + 1) * 512],
                                     start=(fc == 0), stop=(fc == 7), reads=[hFM_b, FQ_b], writes=[MM_b])
                            if oq == 0:
                                S.op("vector", "tensor_copy", qtm[:], MM[:], reads=[MM_b], writes=[qtm_b])
                            elif oq == 1:
                                S.op("vector", "tensor_copy", ktm[:], MM[:], reads=[MM_b], writes=[ktm_b])
                            else:
                                S.op("scalar", "copy", vTM[:, slot * 2048 + tt * 512:slot * 2048 + (tt + 1) * 512],
                                     MM[:], reads=[MM_b], writes=[vTM_b])
                        for (ttm, ttm_b, tbl_c, tbl_s, scl, scl_b, obf, obf_b) in (
                                (qtm, qtm_b, rqc, rqs, QS, QS_b, qbf, qbf_b),
                                (ktm, ktm_b, rkc, rks, KS, KS_b, kbf, kbf_b)):
                            S.op("vector", "tensor_mul", tb1[:], ttm[:], ttm[:], reads=[ttm_b], writes=[tb1_b])
                            for h8 in range(8):
                                S.op("vector", "reduce_sum", n2[:, h8:h8 + 1], tb1[:, h8 * 64:(h8 + 1) * 64], AX,
                                     reads=[tb1_b], writes=[n2_b])
                            S.op("scalar", "activation", n2b[:], n2[:], AF.Sqrt, bias=CONST[:, 1:2],
                                 reads=[n2_b, CONST_b], writes=[n2b_b])
                            S.op("vector", "reciprocal", n2[:], n2b[:], reads=[n2b_b], writes=[n2_b])
                            for h8 in range(8):
                                S.op("vector", "tensor_scalar_mul", ttm[:, h8 * 64:(h8 + 1) * 64],
                                     ttm[:, h8 * 64:(h8 + 1) * 64], n2[:, h8:h8 + 1],
                                     reads=[ttm_b, n2_b], writes=[ttm_b])
                            S.op("vector", "tensor_mul", ttm[:], ttm[:], scl[:], reads=[ttm_b, scl_b], writes=[ttm_b])
                            S.op("scalar", "mul", rot[:].rearrange("p (h t f) -> p h t f", h=8, t=2)[:, :, 0, :],
                                 ttm[:].rearrange("p (h t f) -> p h t f", h=8, t=2)[:, :, 1, :], -1.0,
                                 reads=[ttm_b], writes=[rot_b])
                            S.op("scalar", "copy", rot[:].rearrange("p (h t f) -> p h t f", h=8, t=2)[:, :, 1, :],
                                 ttm[:].rearrange("p (h t f) -> p h t f", h=8, t=2)[:, :, 0, :],
                                 reads=[ttm_b], writes=[rot_b])
                            S.op("sync", "dma_start", out=rc[:], in_=tbl_c[row:row + 128, :], reads=[], writes=[rc_b])
                            S.op("sync", "dma_start", out=rs_[:], in_=tbl_s[row:row + 128, :], reads=[], writes=[rs_b])
                            S.op("vector", "tensor_mul", tb1[:], ttm[:], rc[:], reads=[ttm_b, rc_b], writes=[tb1_b])
                            S.op("vector", "tensor_mul", tb2[:], rot[:], rs_[:], reads=[rot_b, rs_b], writes=[tb2_b])
                            S.op("vector", "tensor_add", obf[:], tb1[:], tb2[:], reads=[tb1_b, tb2_b], writes=[obf_b])
                        for h8 in range(8):
                            tps(qbf[:, h8 * 64:(h8 + 1) * 64], qbf_b,
                                qFM[(h8 % 2) * 64:(h8 % 2) * 64 + 64, (h8 // 2) * 512 + tt * 128:(h8 // 2) * 512 + (tt + 1) * 128],
                                qFM_b, TP, TP_b, h8, ident_t[:], ident_b)
                            tps(kbf[:, h8 * 64:(h8 + 1) * 64], kbf_b,
                                kFM[(h8 % 2) * 64:(h8 % 2) * 64 + 64, slot * 2048 + (h8 // 2) * 512 + tt * 128:slot * 2048 + (h8 // 2) * 512 + (tt + 1) * 128],
                                kFM_b, TP, TP_b, h8 + 1, ident_t[:], ident_b)
                    mset = 4 if c == 0 else 0
                    for h in range(8):
                        hp = h // 2
                        po = (h % 2) * 64
                        for r in range(4):
                            S.op("tensor", "matmul", SS[:, 0:512], qFM[po:po + 64, hp * 512 + r * 128:hp * 512 + (r + 1) * 128],
                                 kFM[po:po + 64, pslot * 2048 + hp * 512:pslot * 2048 + (hp + 1) * 512],
                                 start=True, stop=True, reads=[qFM_b, kFM_b], writes=[SS_b])
                            S.op("tensor", "matmul", SS[:, 512:1024], qFM[po:po + 64, hp * 512 + r * 128:hp * 512 + (r + 1) * 128],
                                 kFM[po:po + 64, slot * 2048 + hp * 512:slot * 2048 + (hp + 1) * 512],
                                 start=True, stop=True, reads=[qFM_b, kFM_b], writes=[SS_b])
                            mc = (mset + r) * 1024
                            S.op("vector", "tensor_add", sm[:, 0:512], SS[:, 0:512], MSK[:, mc:mc + 512],
                                 reads=[SS_b, MSK_b], writes=[sm_b])
                            S.op("vector", "tensor_add", sm[:, 512:1024], SS[:, 512:1024], MSK[:, mc + 512:mc + 1024],
                                 reads=[SS_b, MSK_b], writes=[sm_b])
                            S.op("vector", "reduce_max", stt[:, 0:1], sm[:], AX, reads=[sm_b], writes=[stt_b])
                            S.op("scalar", "mul", stt[:, 1:2], stt[:, 0:1], -1.0, reads=[stt_b], writes=[stt_b])
                            S.op("scalar", "activation", ebf[:], sm[:], AF.Exp, bias=stt[:, 1:2],
                                 reads=[sm_b, stt_b], writes=[ebf_b])
                            S.op("vector", "reduce_sum", stt[:, 2:3], ebf[:], AX, reads=[ebf_b], writes=[stt_b])
                            S.op("vector", "reciprocal", stt[:, 3:4], stt[:, 2:3], reads=[stt_b], writes=[stt_b])
                            for kk in range(8):
                                tps(ebf[:, kk * 128:(kk + 1) * 128], ebf_b, aT[:, kk * 128:(kk + 1) * 128], aT_b,
                                    TP, TP_b, kk, ident_t[:], ident_b)
                            for kk in range(8):
                                vcol = (pslot if kk < 4 else slot) * 2048 + (kk % 4) * 512 + h * 64
                                S.op("tensor", "matmul", OO[:, 0:64], aT[:, kk * 128:(kk + 1) * 128],
                                     vTM[:, vcol:vcol + 64], start=(kk == 0), stop=(kk == 7),
                                     reads=[aT_b, vTM_b], writes=[OO_b])
                            S.op("vector", "tensor_scalar_mul", ao[:, r * 512 + h * 64:r * 512 + (h + 1) * 64],
                                 OO[:, 0:64], stt[:, 3:4], reads=[OO_b, stt_b], writes=[ao_b])
                    for tt in range(4):
                        for fc in range(4):
                            tps(ao[:, tt * 512 + fc * 128:tt * 512 + (fc + 1) * 128], ao_b,
                                aoFM[:, fc * 512 + tt * 128:fc * 512 + (tt + 1) * 128], aoFM_b,
                                TP, TP_b, fc, ident_t[:], ident_b)
                    for tt in range(4):
                        for hf in range(2):
                            for fc in range(4):
                                S.op("tensor", "matmul", MM[:], aoFM[:, fc * 512 + tt * 128:fc * 512 + (tt + 1) * 128],
                                     FO[:, fc * 1024 + hf * 512:fc * 1024 + (hf + 1) * 512],
                                     start=(fc == 0), stop=(fc == 3), reads=[aoFM_b, FO_b], writes=[MM_b])
                            S.op("vector", "tensor_add", yt[:, hf * 512:(hf + 1) * 512],
                                 xch[:, tt * 1024 + hf * 512:tt * 1024 + (hf + 1) * 512], MM[:],
                                 reads=[xch_b, MM_b], writes=[yt_b])
                        row = c * 512 + tt * 128
                        S.op("gpsimd", "dma_start", out=out_d[row:row + 128, :], in_=yt[:], reads=[yt_b], writes=[])
                S.barrier()
                S.replay(block, sems)

        def f_mlp1(l, in_d):
            W = fl[l]
            with ExitStack() as ph:
                sb = lambda n, shp, dt=F32: ph.enter_context(nc.sbuf_tensor(n + f"_m{l}", shp, dt))
                ps = lambda n, shp, dt=F32: ph.enter_context(nc.psum_tensor(n + f"_m{l}", shp, dt))
                WA = sb("WA", [128, 8 * 2816], BF16); WA_b = Buf()
                WG = sb("WG", [128, 8 * 2816], BF16); WG_b = Buf()
                LG = sb("LG", [128, D]); LG_b = Buf()
                LB = sb("LB", [128, D]); LB_b = Buf()
                BA = sb("BA", [128, 22]); BA_b = Buf()
                BG = sb("BG", [128, 22]); BG_b = Buf()
                with ExitStack() as wl:
                    stag = wl.enter_context(nc.sbuf_tensor(f"stg4_{l}", [128, 2816], F32)); stag_b = Buf()
                    load_w_bf16(W["w1a"], 8, 2816, lambda pc: WA[:, pc * 2816:(pc + 1) * 2816], stag, stag_b, WA_b)
                    load_w_bf16(W["w1g"], 8, 2816, lambda pc: WG[:, pc * 2816:(pc + 1) * 2816], stag, stag_b, WG_b)
                    S.op("sync", "dma_start", out=LG[:], in_=W["lg"][:, :], reads=[], writes=[LG_b])
                    S.op("sync", "dma_start", out=LB[:], in_=W["lb"][:, :], reads=[], writes=[LB_b])
                    S.op("sync", "dma_start", out=BA[:], in_=W["b1a"][:, :], reads=[], writes=[BA_b])
                    S.op("sync", "dma_start", out=BG[:], in_=W["b1g"][:, :], reads=[], writes=[BG_b])
                    S.barrier()
                xt = sb("xt4", [128, D]); xt_b = Buf()
                xsh = sb("xsh4", [128, 512]); xsh_b = Buf()
                hh = sb("hh4", [128, D]); hh_b = Buf()
                xc = sb("xc4", [128, D]); xc_b = Buf()
                sq = sb("sq4", [128, D]); sq_b = Buf()
                stt = sb("stt4", [128, 4]); stt_b = Buf()
                hb = sb("hb4", [128, D], BF16); hb_b = Buf()
                hFM = sb("hFM4", [128, 8 * 512], BF16); hFM_b = Buf()
                ga = sb("ga4", [128, 512], BF16); ga_b = Buf()
                ab = sb("ab4", [128, 512]); ab_b = Buf()
                mFM = sb("mFM4", [128, 22 * 512], BF16); mFM_b = Buf()
                A1 = ps("A1", [128, 512], F32); A1_b = Buf()
                G1p = ps("G1p", [128, 512], F32); G1p_b = Buf()
                TP = [ps("TP04", [128, 128], BF16), ps("TP14", [128, 128], BF16)]
                TP_b = [Buf(), Buf()]
                for it in range(NCH):
                    for tt in range(4):
                        row = it * 512 + tt * 128
                        S.op("sync", "dma_start", out=xt[:], in_=in_d[row:row + 128, :], reads=[], writes=[xt_b])
                        if row == 0:
                            S.op("vector", "memset", xsh[0:1, :], 0.0, reads=[], writes=[xsh_b])
                            S.op("sync", "dma_start", out=xsh[1:128, :], in_=in_d[0:127, 512:1024],
                                 reads=[], writes=[xsh_b])
                        else:
                            S.op("sync", "dma_start", out=xsh[:], in_=in_d[row - 1:row + 127, 512:1024],
                                 reads=[], writes=[xsh_b])
                        S.op("vector", "tensor_copy", hh[:, 0:512], xt[:, 0:512], reads=[xt_b], writes=[hh_b])
                        S.op("vector", "tensor_copy", hh[:, 512:1024], xsh[:], reads=[xsh_b], writes=[hh_b])
                        ln(hh[:], hh_b, hb[:], hb_b, xc[:], xc_b, sq[:], sq_b, stt[:], stt_b,
                           gain=LG[:], gain_buf=LG_b, bias=LB[:], bias_buf=LB_b)
                        for fc in range(8):
                            tps(hb[:, fc * 128:(fc + 1) * 128], hb_b, hFM[:, fc * 512 + tt * 128:fc * 512 + (tt + 1) * 128],
                                hFM_b, TP, TP_b, fc, ident_t[:], ident_b)
                    for oc in range(22):
                        for fc in range(8):
                            S.op("tensor", "matmul", A1[:], WA[:, fc * 2816 + oc * 128:fc * 2816 + (oc + 1) * 128],
                                 hFM[:, fc * 512:(fc + 1) * 512], start=(fc == 0), stop=(fc == 7),
                                 reads=[WA_b, hFM_b], writes=[A1_b])
                        for fc in range(8):
                            S.op("tensor", "matmul", G1p[:], WG[:, fc * 2816 + oc * 128:fc * 2816 + (oc + 1) * 128],
                                 hFM[:, fc * 512:(fc + 1) * 512], start=(fc == 0), stop=(fc == 7),
                                 reads=[WG_b, hFM_b], writes=[G1p_b])
                        S.op("scalar", "activation", ga[:], G1p[:], AF.Gelu, bias=BG[:, oc:oc + 1],
                             reads=[G1p_b, BG_b], writes=[ga_b])
                        S.op("vector", "tensor_scalar_add", ab[:], A1[:], BA[:, oc:oc + 1],
                             reads=[A1_b, BA_b], writes=[ab_b])
                        S.op("vector", "tensor_mul", mFM[:, oc * 512:(oc + 1) * 512], ab[:], ga[:],
                             reads=[ab_b, ga_b], writes=[mFM_b])
                    S.op("gpsimd", "dma_start", out=m_d[it, :, :], in_=mFM[:], reads=[mFM_b], writes=[])
                S.barrier()
                S.replay(block, sems)

        def f_mlp2(l, res_d, out_d, add_d=None):
            W = fl[l]
            with ExitStack() as ph:
                sb = lambda n, shp, dt=F32: ph.enter_context(nc.sbuf_tensor(n + f"_n{l}", shp, dt))
                ps = lambda n, shp, dt=F32: ph.enter_context(nc.psum_tensor(n + f"_n{l}", shp, dt))
                W2f = sb("W2f", [128, 22 * 1024], BF16); W2f_b = Buf()
                B2f = sb("B2f", [128, D]); B2f_b = Buf()
                with ExitStack() as wl:
                    stag = wl.enter_context(nc.sbuf_tensor(f"stg5_{l}", [128, 1024], F32)); stag_b = Buf()
                    load_w_bf16(W["w2"], 22, 1024, lambda pc: W2f[:, pc * 1024:(pc + 1) * 1024], stag, stag_b, W2f_b)
                    S.op("sync", "dma_start", out=B2f[:], in_=W["b2"][:, :], reads=[], writes=[B2f_b])
                    S.barrier()
                mFM = sb("mFM5", [128, 22 * 512], BF16); mFM_b = Buf()
                xt = sb("xt5", [128, 4 * D]); xt_b = Buf()
                xadd = sb("xadd5", [128, D]); xadd_b = Buf()
                M2 = ps("M25", [128, 512], F32); M2_b = Buf()
                for it in range(NCH):
                    S.op("sync", "dma_start", out=mFM[:], in_=m_d[it, :, :], reads=[], writes=[mFM_b])
                    for tt in range(4):
                        row = it * 512 + tt * 128
                        S.op("sync", "dma_start", out=xt[:, tt * D:(tt + 1) * D], in_=res_d[row:row + 128, :],
                             reads=[], writes=[xt_b])
                        S.op("vector", "tensor_add", xt[:, tt * D:(tt + 1) * D], xt[:, tt * D:(tt + 1) * D], B2f[:],
                             reads=[xt_b, B2f_b], writes=[xt_b])
                        if add_d is not None:
                            S.op("sync", "dma_start", out=xadd[:, 0:D], in_=add_d[row:row + 128, :],
                                 reads=[], writes=[xadd_b])
                            S.op("vector", "tensor_add", xt[:, tt * D:(tt + 1) * D], xt[:, tt * D:(tt + 1) * D],
                                 xadd[:, 0:D], reads=[xt_b, xadd_b], writes=[xt_b])
                    for tt in range(4):
                        row = it * 512 + tt * 128
                        for hf in range(2):
                            for fc in range(22):
                                S.op("tensor", "matmul", M2[:], mFM[:, fc * 512 + tt * 128:fc * 512 + (tt + 1) * 128],
                                     W2f[:, fc * 1024 + hf * 512:fc * 1024 + (hf + 1) * 512],
                                     start=(fc == 0), stop=(fc == 21), reads=[mFM_b, W2f_b], writes=[M2_b])
                            S.op("vector", "tensor_add", xt[:, tt * D + hf * 512:tt * D + (hf + 1) * 512],
                                 xt[:, tt * D + hf * 512:tt * D + (hf + 1) * 512], M2[:],
                                 reads=[xt_b, M2_b], writes=[xt_b])
                        S.op("gpsimd", "dma_start", out=out_d[row:row + 128, :], in_=xt[:, tt * D:(tt + 1) * D],
                             reads=[xt_b], writes=[])
                S.barrier()
                S.replay(block, sems)

        f_attn(0, xi_d, ya_d)
        f_mlp1(0, ya_d)
        f_mlp2(0, ya_d, z_d)
        f_attn(1, z_d, ya_d)
        f_mlp1(1, ya_d)
        f_mlp2(1, ya_d, y_out, add_d=xA_d)

    es.close()
    return nc


def kernel(**inp):
    x = np.asarray(inp["L0_attn_x"], np.float32)

    def bc(v):
        return np.ascontiguousarray(np.broadcast_to(np.asarray(v, np.float32), (128, v.shape[-1])))

    wqkvT = np.ascontiguousarray(np.asarray(inp["L0_attn_w_qkv"], np.float32).T)
    woT = np.ascontiguousarray(np.asarray(inp["L0_attn_w_o"], np.float32).T)
    w1T = np.ascontiguousarray(np.asarray(inp["L0_attn_mlp_w1"], np.float32).T)
    w2T = np.ascontiguousarray(np.asarray(inp["L0_attn_mlp_w2"], np.float32).T)
    b1A = np.ascontiguousarray(np.asarray(inp["L0_attn_mlp_b1"], np.float32).reshape(32, 128).T)

    ii = np.arange(512)[:, None]
    jj = np.arange(1024)[None, :]
    bandA = (jj >= ii + 1) & (jj <= ii + 512)
    mA = np.where(bandA, 0.0, NEG).astype(np.float32)
    mA0 = np.where(bandA & (jj >= 512), 0.0, NEG).astype(np.float32)
    bandF = (jj >= ii) & (jj <= ii + 512)
    mF = np.where(bandF, 0.0, NEG).astype(np.float32)
    mF0 = np.where(bandF & (jj >= 512), 0.0, NEG).astype(np.float32)
    maskA = np.concatenate([mA.reshape(4, 128, 1024), mA0.reshape(4, 128, 1024)], 0).astype(ml_dtypes.bfloat16)
    maskF = np.concatenate([mF.reshape(4, 128, 1024), mF0.reshape(4, 128, 1024)], 0).astype(ml_dtypes.bfloat16)
    identH = np.eye(128, dtype=ml_dtypes.bfloat16)

    n = 16384
    dh = 64
    half = np.arange(0, dh, 2, dtype=np.float64)
    inv_freq = 1.0 / 10000.0 ** (half / dh)
    t = np.arange(n, dtype=np.float64)
    fr = t[:, None] * inv_freq[None, :]
    fr = np.concatenate([fr, fr], -1)
    base = (half + 0.4 * dh) / (1.4 * dh)
    power = (t - n // 2) / float(256)
    sc = base[None, :] ** power[:, None]
    sc = np.concatenate([sc, sc], -1)
    cos = np.cos(fr); sin = np.sin(fr)
    qc = (cos * sc * 8.0).astype(np.float32)
    qs = (sin * sc * 8.0).astype(np.float32)
    kc = (cos / sc).astype(np.float32)
    ks = (sin / sc).astype(np.float32)
    tile8 = lambda a: np.ascontiguousarray(np.tile(a, (1, 8)))
    qc, qs, kc, ks = tile8(qc), tile8(qs), tile8(kc), tile8(ks)

    common = dict(
        wqkvT=wqkvT, woT=woT,
        g1b=bc(np.asarray(inp["L0_attn_ln1_g"], np.float32)), b1b=bc(np.asarray(inp["L0_attn_ln1_b"], np.float32)),
        g2b=bc(np.asarray(inp["L0_attn_ln2_g"], np.float32)), b2b=bc(np.asarray(inp["L0_attn_ln2_b"], np.float32)),
        w1T=w1T, b1A=b1A, w2T=w2T, b2bc=bc(np.asarray(inp["L0_attn_mlp_b2"], np.float32)),
        maskA=maskA, maskF=maskF, identH=identH,
    )
    fw1 = np.asarray(inp["L0_ffn_w1"], np.float32)
    fw2 = np.asarray(inp["L0_ffn_w2"], np.float32)
    for l in range(2):
        w1a = np.zeros((1024, 2816), np.float32); w1a[:, :2730] = fw1[l][:, :2730]
        w1g = np.zeros((1024, 2816), np.float32); w1g[:, :2730] = fw1[l][:, 2730:]
        b1 = np.asarray(inp["L0_ffn_b1"], np.float32)[l]
        b1a = np.zeros(2816, np.float32); b1a[:2730] = b1[:2730]
        b1g = np.zeros(2816, np.float32); b1g[:2730] = b1[2730:]
        w2p = np.zeros((2816, 1024), np.float32); w2p[:2730] = fw2[l]
        common.update({
            f"fqkv{l}": np.ascontiguousarray(np.asarray(inp["L0_ffn_wqkv"], np.float32)[l]),
            f"fqs{l}": bc(np.tile(np.asarray(inp["L0_ffn_q_scale"], np.float32)[l], 8)),
            f"fks{l}": bc(np.tile(np.asarray(inp["L0_ffn_k_scale"], np.float32)[l], 8)),
            f"fwo{l}": np.ascontiguousarray(np.asarray(inp["L0_ffn_wo"], np.float32)[l]),
            f"flg{l}": bc(np.asarray(inp["L0_ffn_ln_g"], np.float32)[l]),
            f"flb{l}": bc(np.asarray(inp["L0_ffn_ln_b"], np.float32)[l]),
            f"fw1a{l}": w1a, f"fw1g{l}": w1g,
            f"fb1a{l}": np.ascontiguousarray(b1a.reshape(22, 128).T),
            f"fb1g{l}": np.ascontiguousarray(b1g.reshape(22, 128).T),
            f"fw2{l}": w2p,
            f"fb2bc{l}": bc(np.asarray(inp["L0_ffn_b2"], np.float32)[l]),
        })

    out_start = [0, 5632, 9216, 12800]
    comp_start = [0, 3584, 7168, 10752]
    in_maps = []
    for core in range(8):
        b, q = core // 4, core % 4
        m = dict(common)
        cs = comp_start[q]
        m["x"] = np.ascontiguousarray(x[b, cs:cs + T, :])
        gs = cs
        m["rqc"] = np.ascontiguousarray(qc[gs:gs + T])
        m["rqs"] = np.ascontiguousarray(qs[gs:gs + T])
        m["rkc"] = np.ascontiguousarray(kc[gs:gs + T])
        m["rks"] = np.ascontiguousarray(ks[gs:gs + T])
        in_maps.append(m)

    nc = build_program()
    res = run_bass_kernel_spmd(nc, in_maps, core_ids=list(range(8)))
    out = np.empty((2, 16384, 1024), np.float32)
    for core in range(8):
        b, q = core // 4, core % 4
        yc = np.asarray(res.results[core]["y"])
        skip = 0 if q == 0 else 2048
        out[b, out_start[q]:out_start[q] + (T - skip), :] = yc[skip:]
    return out
